# version 04
# baseline (speedup 1.0000x reference)
import contextlib
import numpy as np
import concourse.bass as bass
import concourse.mybir as mybir

F32 = mybir.dt.float32
I32 = mybir.dt.int32
BF16 = mybir.dt.bfloat16
ALU = mybir.AluOpType
ACT = mybir.ActivationFunctionType
AX = mybir.AxisListType

HID = 2048
TKV = 4096
NH = 16
DH = 128
II = 8192
NQC = 16
NKC = 32
NDMC = 16
NIC = 64
EPS = 1e-5
DSC = float(DH) ** -0.5

ENGINES = ("pe", "act", "dve", "pool", "sp")


class Op:
    __slots__ = ("eng", "fn", "reads", "writes", "dma", "sem_grp", "bulk", "idx",
                 "waits", "need_inc", "deps", "inc_val", "same_max", "nofence")

    def __init__(self, eng, fn, reads, writes, dma, sem_grp=None, bulk=False):
        self.eng = eng
        self.fn = fn
        self.reads = list(reads)
        self.writes = list(writes)
        self.dma = dma
        self.sem_grp = sem_grp
        self.bulk = bulk
        self.idx = None
        self.waits = []
        self.need_inc = False
        self.deps = set()
        self.inc_val = 0
        self.same_max = 0
        self.nofence = False


class Sched:

    def __init__(self):
        self.ops = []
        self.last_w = {}
        self.readers = {}
        self.ngrp = {}
        self.neng = {e: 0 for e in ENGINES}
        self.fence_snapshot = []
        self.last_op = {e: None for e in ENGINES}
        self.nofence_groups = set()

    def op(self, eng, fn, reads=(), writes=(), dma=None, sem_grp=None, bulk=False,
           nofence=False):
        o = Op(eng, fn, reads, writes, dma, sem_grp, bulk)
        o.nofence = nofence
        if nofence and dma is not None:
            self.nofence_groups.add(sem_grp)
        if dma is not None:
            self.ngrp[sem_grp] = self.ngrp.get(sem_grp, 0) + 1
            o.idx = self.ngrp[sem_grp]
        else:
            self.neng[eng] += 1
            o.idx = self.neng[eng]
        deps = set()
        for r in o.reads:
            w = self.last_w.get(r)
            if w is not None:
                deps.add(w)
        for r in o.writes:
            w = self.last_w.get(r)
            if w is not None:
                deps.add(w)
            for rd in self.readers.get(r, ()):
                deps.add(rd)
        if not nofence:
            for fop in self.fence_snapshot:
                deps.add(fop)
        o.deps = deps
        self.ops.append(o)
        for r in o.writes:
            self.last_w[r] = o
            self.readers[r] = []
        for r in o.reads:
            self.readers.setdefault(r, []).append(o)
        if dma is None:
            self.last_op[eng] = o
        return o

    def dma(self, eng, fn, reads=(), writes=(), sem=None, bulk=False, nofence=False):
        grp = sem if sem is not None else (writes[0] if writes else "misc")
        return self.op(eng, fn, reads, writes, dma=eng, sem_grp=grp, bulk=bulk,
                       nofence=nofence)

    def fence(self, include_nofence=False):
        snap = []
        for e in ENGINES:
            if self.last_op[e] is not None:
                snap.append(self.last_op[e])
        seen = set()
        for o in reversed(self.ops):
            if o.dma is not None and o.sem_grp not in seen:
                if o.sem_grp in self.nofence_groups and not include_nofence:
                    continue
                seen.add(o.sem_grp)
                snap.append(o)
        self.fence_snapshot = snap
        for o in snap:
            if o.dma is not None:
                o.bulk = True

    def resolve(self):
        totals = dict(self.ngrp)
        for o in self.ops:
            for d in o.deps:
                if d is o:
                    continue
                if d.dma is not None:
                    v = 16 * (totals[d.sem_grp] if d.bulk else d.idx)
                    o.waits.append((("g", d.sem_grp), v))
                elif d.eng != o.eng:
                    d.need_inc = True
                    o.waits.append((("eng", d.eng), d))
                else:
                    if o.eng in ("dve", "act"):
                        o.same_max = max(o.same_max, d.idx)
        cnt = {e: 0 for e in ENGINES}
        for o in self.ops:
            if o.dma is None and o.need_inc:
                cnt[o.eng] += 1
            o.inc_val = cnt[o.eng]
        for o in self.ops:
            o.waits = [(k, (v if isinstance(v, int) else v.inc_val)) for k, v in o.waits]

    def final_waits(self):
        return [(("g", g), 16 * n) for g, n in self.ngrp.items()]

    def emit(self, nc, es):
        self.resolve()
        sems = {("eng", e): es.enter_context(nc.semaphore("sem_" + e)) for e in ENGINES}
        for g in self.ngrp:
            sems[("g", g)] = es.enter_context(nc.semaphore("sg_" + g))
        per_eng = {e: [] for e in ENGINES}
        for o in self.ops:
            per_eng[o.eng].append(o)
        block = es.enter_context(nc.Block())

        def mk(eng_name):
            ops = per_eng[eng_name]

            def prog(eng):
                hi = {}
                drained = 0
                for o in ops:
                    for k, v in o.waits:
                        if v > hi.get(k, 0):
                            hi[k] = v
                            eng.wait_ge(sems[k], v)
                    if o.fn is None:
                        continue
                    if o.same_max > drained:
                        eng.drain()
                        drained = o.idx - 1
                    inst = o.fn(eng)
                    if o.dma is not None:
                        inst.then_inc(sems[("g", o.sem_grp)], 16)
                    elif o.need_inc:
                        inst.then_inc(sems[("eng", o.eng)], 1)
            return prog

        block.tensor(mk("pe"))
        block.scalar(mk("act"))
        block.vector(mk("dve"))
        block.gpsimd(mk("pool"))
        block.sync(mk("sp"))


def quant_stages(b, tag, src_ap, src_res, dq_ap, dq_res, C):
    nb = C // 32
    B = lambda n: b[n][:, :C]
    Bb = lambda n: b[n][:, :nb]
    rs = lambda *names: [tag + n for n in names]

    def st_red(v):
        s3 = src_ap.rearrange("p (b w) -> p b w", w=32)
        return v.tensor_reduce(Bb("m"), s3, AX.X, ALU.max, apply_absolute_value=True)

    def st_s(v):
        return v.tensor_scalar(Bb("s"), Bb("m"), 1.0 / 448.0, 1e-30, ALU.mult, ALU.max)

    def st_si(v):
        return v.reciprocal(Bb("sinv"), Bb("s"))

    def st_y(v):
        s3 = src_ap.rearrange("p (b w) -> p b w", w=32)
        sb = Bb("sinv").unsqueeze(2).broadcast_to([128, nb, 32])
        return v.tensor_tensor(B("y").rearrange("p (b w) -> p b w", w=32), s3, sb, ALU.mult)

    def st_ei(v):
        return v.tensor_scalar(B("ei"), B("y").bitcast(I32), 23, 255,
                               ALU.logical_shift_right, ALU.bitwise_and)

    def st_ub(v):
        return v.tensor_scalar(B("ub"), B("ei"), 121.0, 8388608.0, ALU.max, ALU.mult)

    def st_ui(v):
        return v.tensor_scalar(B("ui"), B("ub"), -1.0, 2155872256.0, ALU.mult, ALU.add)

    def st_ri(v):
        return v.tensor_tensor(B("ri"), B("y"), B("ui").bitcast(F32), ALU.mult)

    def st_rf(sc):
        return sc.activation(B("rf"), B("ri"), ACT.Copy, bias=0.0, scale=0.125)

    def st_q(v):
        return v.tensor_tensor(B("q"), B("rf"), B("ub").bitcast(F32), ALU.mult)

    def st_dq(v):
        sb2 = Bb("s").unsqueeze(2).broadcast_to([128, nb, 32])
        return v.tensor_tensor(dq_ap.rearrange("p (b w) -> p b w", w=32),
                               B("q").rearrange("p (b w) -> p b w", w=32), sb2, ALU.mult)

    return [
        ("dve", st_red, [src_res], rs("m")),
        ("dve", st_s, rs("m"), rs("s")),
        ("dve", st_si, rs("s"), rs("sinv")),
        ("dve", st_y, [src_res] + rs("sinv"), rs("y")),
        ("dve", st_ei, rs("y"), rs("ei")),
        ("dve", st_ub, rs("ei"), rs("ub")),
        ("dve", st_ui, rs("ub"), rs("ui")),
        ("dve", st_ri, rs("y", "ui"), rs("ri")),
        ("act", st_rf, rs("ri"), rs("rf")),
        ("dve", st_q, rs("rf", "ub"), rs("q")),
        ("dve", st_dq, rs("q", "s"), [dq_res]),
    ]


def emit_quant_multi(S, jobs, nofence=False):
    stages = [quant_stages(*j) for j in jobs]
    for si in range(11):
        for st in stages:
            eng, fn, rd, wr = st[si]
            S.op(eng, fn, reads=rd, writes=wr, nofence=nofence)


def emit_quant(S, b, tag, src_ap, src_res, dq_ap, dq_res, C):
    emit_quant_multi(S, [(b, tag, src_ap, src_res, dq_ap, dq_res, C)])


def alloc_quant_bufs(es, nc, C, tag):
    sb = lambda n, dt, w=C: es.enter_context(nc.sbuf_tensor(tag + n, [128, w], dt))
    b = {}
    b["m"] = sb("m", F32, C // 32)
    b["s"] = sb("s", F32, C // 32)
    b["sinv"] = sb("si", F32, C // 32)
    b["y"] = sb("y", F32)
    b["ei"] = sb("ei", I32)
    b["ub"] = sb("ub", I32)
    b["ui"] = sb("ui", I32)
    b["ri"] = sb("ri", I32)
    b["rf"] = sb("rf", F32)
    b["q"] = sb("q", F32)
    return b


def build_program(cfg=None):
    cfg = cfg or {}
    nc = bass.Bass("TRN2", target_bir_lowering=False)
    root = contextlib.ExitStack()
    S = Sched()
    phases = cfg.get("phases", ("dequant", "v", "attn", "wo", "ffn"))
    tap = cfg.get("tap", None)

    xt = nc.dram_tensor("xt", [HID, TKV], F32, kind="ExternalInput")
    xqf = nc.dram_tensor("xqf", [HID, 2048], F32, kind="ExternalInput")
    xqt = nc.dram_tensor("xqt", [2048, HID], F32, kind="ExternalInput")
    wq = nc.dram_tensor("wqT", [HID, HID], F32, kind="ExternalInput")
    wk = nc.dram_tensor("wkT", [HID, HID], F32, kind="ExternalInput")
    wv = nc.dram_tensor("wvT", [HID, HID], F32, kind="ExternalInput")
    wo = nc.dram_tensor("woT", [HID, HID], F32, kind="ExternalInput")
    gwd = nc.dram_tensor("gw", [II, HID], F32, kind="ExternalInput")
    uwd = nc.dram_tensor("uw", [II, HID], F32, kind="ExternalInput")
    dwd = nc.dram_tensor("dw", [HID, II], F32, kind="ExternalInput")
    gb2 = nc.dram_tensor("gb2", [128, 512], F32, kind="ExternalInput")
    bb2 = nc.dram_tensor("bb2", [128, 512], F32, kind="ExternalInput")
    mAd = nc.dram_tensor("mA", [128, 128], F32, kind="ExternalInput")
    mBd = nc.dram_tensor("mB", [128, 128], F32, kind="ExternalInput")
    idin = nc.dram_tensor("idin", [128, 128], F32, kind="ExternalInput")
    out = nc.dram_tensor("outp", [2048, HID], F32, kind="ExternalOutput")

    ik = lambda t: "ExternalOutput" if tap == t else "Internal"
    gws = nc.dram_tensor("gws", [NDMC, 128, II], BF16, kind=ik("dq"))
    uws = nc.dram_tensor("uws", [NDMC, 128, II], BF16, kind=ik("dq"))
    dws = nc.dram_tensor("dws", [NIC, 128, HID], BF16, kind=ik("dq"))
    vsp = nc.dram_tensor("vsp", [NH, NKC, 128, 128], BF16, kind=ik("v"))
    osp = nc.dram_tensor("osp", [NH, 128, 2048], BF16,
                         kind=("ExternalOutput" if tap in ("o", "azo") else "Internal"))
    x1s = nc.dram_tensor("x1s", [NQC, 128, HID], F32, kind=ik("x1"))
    kqd = nc.dram_tensor("kqd", [8, 128, TKV], BF16, kind=ik("kq"))
    azd = nc.dram_tensor("azd", [4, 128, 512], F32,
                         kind=("ExternalOutput" if tap in ("az", "azo") else "Internal"))

    gsb = lambda es_, n, shape, dt: es_.enter_context(nc.sbuf_tensor(n, shape, dt))
    PB = [root.enter_context(nc.psum_tensor("pb%d" % i, [128, 512], F32)) for i in range(8)]
    PBv = [PB[i][:, :].bitcast(BF16) for i in range(8)]

    ident16 = gsb(root, "ident16", [128, 128], BF16)
    S.dma("pool", lambda e: e.dma_start(out=ident16[:, :], in_=idin[:, :]),
          reads=[], writes=["ident16"])

    tcnt = [0]

    def transpose_to(src_ap_list, src_res, dst_ap_of, dst_res):
        n = len(src_ap_list)
        for base in range(0, n, 4):
            m = min(4, n - base)
            bank = 6 + (tcnt[0] % 2)
            tcnt[0] += 1
            pres = "pb%d" % bank
            for t in range(m):
                def tr(e, t=t, base=base, bank=bank):
                    return e.transpose(PBv[bank][:, (t * 128):(t * 128 + 128)],
                                       src_ap_list[base + t], ident16[:, :])
                S.op("pe", tr, reads=[src_res, "ident16"],
                     writes=[pres] if t == m - 1 else [pres])

            def cp(e, base=base, m=m, bank=bank):
                return e.tensor_copy(dst_ap_of(base, m), PBv[bank][:, 0:m * 128])
            S.op("dve", cp, reads=[pres], writes=[dst_res])

    dq_gen_obj = None
    dq_es = contextlib.ExitStack()
    if "dequant" in phases:
        sbd = lambda n, shape, dt: gsb(dq_es, n, shape, dt)
        qbs = [alloc_quant_bufs(dq_es, nc, 1024, "qd%d" % i) for i in range(2)]
        raw = [sbd("rawd%d" % i, [128, 2048], F32) for i in range(2)]
        dqc = [sbd("dqcb%d" % i, [128, 1024], BF16) for i in range(2)]
        band = sbd("band", [128, 16 * 512], BF16)

        def dq_gen():
            for wname, win, wsp, nrc, nslab in (("gw", gwd, gws, 64, 1), ("uw", uwd, uws, 64, 1),
                                                ("dw", dwd, dws, 16, 4)):
                for slab in range(nslab):
                    for bandi in range(nrc // 4):
                        for rc4 in range(4):
                            rc = bandi * 4 + rc4
                            i = rc % 2
                            src = win[rc * 128:(rc + 1) * 128, slab * 2048:(slab + 1) * 2048]

                            def ld(e, src=src, i=i):
                                return e.dma_start(out=raw[i][:, :], in_=src)
                            S.dma("sp", ld, reads=[], writes=["rawd%d" % i], nofence=True)
                            emit_quant_multi(S, [
                                (qbs[h], "qd%d" % h, raw[i][:, h * 1024:(h + 1) * 1024],
                                 "rawd%d" % i, dqc[h][:, :], "dqcq%d" % h, 1024)
                                for h in range(2)], nofence=True)
                            for h in range(2):
                                for t in range(8):
                                    dchunk = h * 8 + t
                                    bank = 6 + (tcnt[0] % 2)
                                    tcnt[0] += 1
                                    pres = "pb%d" % bank

                                    def tr(e, h=h, t=t, bank=bank):
                                        return e.transpose(PBv[bank][:, 0:128],
                                                           dqc[h][:, t * 128:(t + 1) * 128],
                                                           ident16[:, :])
                                    S.op("pe", tr, reads=["dqcq%d" % h, "ident16"], writes=[pres],
                                         nofence=True)

                                    def cp(e, bank=bank, dchunk=dchunk, rc4=rc4):
                                        return e.tensor_copy(
                                            band[:, dchunk * 512 + rc4 * 128: dchunk * 512 + rc4 * 128 + 128],
                                            PBv[bank][:, 0:128])
                                    S.op("dve", cp, reads=[pres], writes=["band_c%d" % dchunk],
                                         nofence=True)
                            yield
                        for dchunk in range(16):
                            dst = wsp[slab * 16 + dchunk, :, bandi * 512:(bandi + 1) * 512]

                            def sp_(e, dst=dst, dchunk=dchunk):
                                return e.dma_start(out=dst, in_=band[:, dchunk * 512: (dchunk + 1) * 512])
                            S.dma("pool", sp_, reads=["band_c%d" % dchunk],
                                  writes=["%s_sp%d_%d" % (wname, slab * 16 + dchunk, bandi)],
                                  sem="wsps%d" % dchunk, nofence=True)
                        yield
        dq_gen_obj = dq_gen()

    def dq_step(n=1):
        if dq_gen_obj is not None:
            for _ in range(n):
                if next(dq_gen_obj, "END") == "END":
                    break

    if "v" in phases:
        with contextlib.ExitStack() as es:
            sb = lambda n, shape, dt: gsb(es, n, shape, dt)
            wv_sb = sb("wv_sb", [128, NDMC * HID], BF16)
            for dmc in range(NDMC):
                def ldwv(e, dmc=dmc):
                    return e.dma_start(out=wv_sb[:, dmc * HID:(dmc + 1) * HID],
                                       in_=wv[dmc * 128:(dmc + 1) * 128, :])
                S.dma("pool", ldwv, reads=[], writes=["wv_sb"])
            xt_sb = [sb("xt_sb%d" % i, [128, NDMC * 128], BF16) for i in range(2)]
            vstage = [sb("vstage%d" % i, [128, 512], BF16) for i in range(4)]
            for tc in range(NKC):
                i = tc % 2
                src = xt[:, tc * 128:(tc + 1) * 128].rearrange("(c p) f -> p c f", p=128)

                def ldxt(e, src=src, i=i):
                    return e.dma_start(out=xt_sb[i][:, :].rearrange("p (c f) -> p c f", f=128), in_=src)
                S.dma("pool", ldxt, reads=[], writes=["xts%d" % i])
                for hg in range(4):
                    pres = "pb%d" % hg
                    pb = PB[hg]
                    for dmc in range(NDMC):
                        def mm(e, i=i, dmc=dmc, hg=hg, pb=pb):
                            return e.matmul(pb[:, :], xt_sb[i][:, dmc * 128:(dmc + 1) * 128],
                                            wv_sb[:, dmc * HID + hg * 512: dmc * HID + hg * 512 + 512],
                                            start=(dmc == 0), stop=(dmc == NDMC - 1))
                        S.op("pe", mm, reads=["xts%d" % i, "wv_sb"], writes=[pres])
                    vst = (tc % 2) * 2 + hg % 2

                    def ev(e, pb=pb, vst=vst):
                        return e.tensor_copy(vstage[vst][:, :], pb[:, :])
                    S.op("dve", ev, reads=[pres], writes=["vstage%d" % vst])
                    dst = vsp[hg * 4: hg * 4 + 4, tc, :, :]

                    def spv(e, dst=dst, vst=vst):
                        return e.dma_start(out=dst.rearrange("h p f -> p h f"),
                                           in_=vstage[vst][:, :].rearrange("p (h f) -> p h f", f=128))
                    S.dma("pool", spv, reads=["vstage%d" % vst], writes=["vsp_%d_%d" % (tc, hg)],
                          sem="vsps%d" % vst)
                dq_step(1)
        S.fence()

    RH = 2
    if "attn" in phases:
        with contextlib.ExitStack() as es:
            sb = lambda n, shape, dt: gsb(es, n, shape, dt)
            gb_sb = sb("gb_sb", [128, 512], F32)
            bb_sb = sb("bb_sb", [128, 512], F32)
            mA_sb = sb("mA_sb", [128, 128], F32)
            mB_sb = sb("mB_sb", [128, 128], F32)
            S.dma("sp", lambda e: e.dma_start(out=gb_sb[:, :], in_=gb2[:, :]), reads=[], writes=["gb_sb"])
            S.dma("sp", lambda e: e.dma_start(out=bb_sb[:, :], in_=bb2[:, :]), reads=[], writes=["bb_sb"])
            S.dma("sp", lambda e: e.dma_start(out=mA_sb[:, :], in_=mAd[:, :]), reads=[], writes=["mA_sb"])
            S.dma("sp", lambda e: e.dma_start(out=mB_sb[:, :], in_=mBd[:, :]), reads=[], writes=["mB_sb"])

            kfm = [sb("kfm%d" % j, [128, TKV], BF16) for j in range(RH)]
            qfm = [sb("qfm%d" % j, [128, 2048], BF16) for j in range(RH)]
            vaug = [sb("vaug%d" % j, [128, NKC * 129], BF16) for j in range(RH)]
            for j in range(RH):
                def ms(e, j=j):
                    return e.memset(vaug[j][:, :], 1.0)
                S.op("dve", ms, reads=[], writes=["vaug%d" % j])
            wfsb = {"k": sb("wfsb_k", [128, NDMC * 128 * RH], BF16),
                    "q": sb("wfsb_q", [128, NDMC * 128 * RH], BF16)}
            wslc = [sb("wslc%d" % i, [128, 128 * RH], BF16) for i in range(2)]
            xts2 = [sb("xts2_%d" % i, [128, NDMC * 128], BF16) for i in range(2)]
            ygrp = sb("ygrp", [128, 8 * 128 * RH], F32)
            y2scr = sb("y2scr", [128, 128 * RH], F32)
            ktmp = sb("ktmp", [128, 128 * RH], BF16)
            ssum = sb("ssum", [128, 8 * RH], F32)
            ssq = sb("ssq", [128, 8 * RH], F32)
            smu = sb("smu", [128, 8 * RH], F32)
            srs = sb("srs", [128, 8 * RH], F32)
            sscr = sb("sscr", [128, 8 * RH], F32)
            kmu = sb("kmu", [128, NKC * RH], F32)
            krs = sb("krs", [128, NKC * RH], F32)
            qmu = sb("qmu", [128, NQC * RH], F32)
            qrs = sb("qrs", [128, NQC * RH], F32)
            asb = [sb("asb%d" % i, [128, 512], BF16) for i in range(2)]
            onorm = sb("onorm", [128, 128], BF16)
            ofm = [sb("ofm%d" % j, [128, 2048], BF16) for j in range(RH)]
            zeps = sb("zeps", [128, 1], F32)
            zinv = sb("zinv", [128, 1], F32)
            W = 128 * RH

            def proj_fm(r, which, wdram, src_dram, nchunks, dst_list, mubuf, rsbuf):
                for dmc in range(NDMC):
                    i = dmc % 2

                    def ldw(e, dmc=dmc, i=i, wdram=wdram, r=r):
                        return e.dma_start(out=wslc[i][:, :],
                                           in_=wdram[dmc * 128:(dmc + 1) * 128, r * W:(r + 1) * W])
                    S.dma("pool", ldw, reads=["wslc_free%d" % i], writes=["wslc%d" % i])

                    def fold(v, dmc=dmc, i=i, which=which):
                        return v.tensor_tensor(wfsb[which][:, dmc * W:(dmc + 1) * W],
                                               wslc[i][:, :], gb_sb[:, :W], ALU.mult)
                    S.op("dve", fold, reads=["wslc%d" % i, "gb_sb"],
                         writes=["wfsb_" + which, "wslc_free%d" % i])
                ngrp = nchunks // 8
                for grp in range(ngrp):
                    for tcg in range(8):
                        tc = grp * 8 + tcg
                        i = tc % 2
                        ssrc = src_dram[:, tc * 128:(tc + 1) * 128].rearrange("(c p) f -> p c f", p=128)

                        def ldx(e, ssrc=ssrc, i=i):
                            return e.dma_start(out=xts2[i][:, :].rearrange("p (c f) -> p c f", f=128),
                                               in_=ssrc)
                        S.dma("pool", ldx, reads=["xts2free%d" % i], writes=["xts2_%d" % i])
                        pres = "pb%d" % (4 + (tc % 3) if (tc % 3) < 2 else 3)
                        pres = ["pb4", "pb5", "pb3"][tc % 3]
                        pb = PB[[4, 5, 3][tc % 3]]
                        for dmc in range(NDMC):
                            def mm(e, i=i, dmc=dmc, pb=pb, which=which):
                                return e.matmul(pb[:, 0:W], xts2[i][:, dmc * 128:(dmc + 1) * 128],
                                                wfsb[which][:, dmc * W:(dmc + 1) * W],
                                                start=(dmc == 0), stop=(dmc == NDMC - 1))
                            S.op("pe", mm, reads=["xts2_%d" % i, "wfsb_" + which],
                                 writes=[pres, "xts2free%d" % i])
                        for j in range(RH):
                            col = tcg * RH + j

                            def badd(e, pb=pb, j=j, tcg=tcg, col=col):
                                return e.scalar_tensor_tensor(
                                    ygrp[:, tcg * W + j * 128: tcg * W + j * 128 + 128],
                                    pb[:, j * 128:(j + 1) * 128], 1.0,
                                    bb_sb[:, j * 128:(j + 1) * 128], ALU.mult, ALU.add,
                                    accum_out=ssum[:, col:col + 1])
                            S.op("dve", badd, reads=[pres, "bb_sb"], writes=["yg_%d_%d" % (tcg, j)])

                            def sqa(e, tcg=tcg, j=j, col=col):
                                return e.activation(y2scr[:, j * 128:(j + 1) * 128],
                                                    ygrp[:, tcg * W + j * 128: tcg * W + j * 128 + 128],
                                                    ACT.Square, accum_out=ssq[:, col:col + 1])
                            S.op("act", sqa, reads=["yg_%d_%d" % (tcg, j)], writes=["sq_%d_%d" % (tcg, j)])
                    allres = (["yg_%d_%d" % (t_, j_) for t_ in range(8) for j_ in range(RH)]
                              + ["sq_%d_%d" % (t_, j_) for t_ in range(8) for j_ in range(RH)])
                    NC8 = 8 * RH

                    def stats(v, which=which):
                        v.tensor_scalar(smu[:, :NC8], ssum[:, :NC8], 1.0 / 128.0, None, ALU.mult)
                        v.tensor_scalar(sscr[:, :NC8], ssq[:, :NC8], 1.0 / 128.0, None, ALU.mult)
                        v.drain()
                        v.tensor_tensor(srs[:, :NC8], smu[:, :NC8], smu[:, :NC8], ALU.mult)
                        v.drain()
                        v.tensor_tensor(sscr[:, :NC8], sscr[:, :NC8], srs[:, :NC8], ALU.subtract)
                        v.drain()
                        v.tensor_scalar(sscr[:, :NC8], sscr[:, :NC8], EPS, None, ALU.add)
                        v.drain()
                        return v.reciprocal(srs[:, :NC8], sscr[:, :NC8])
                    S.op("dve", stats, reads=allres, writes=["smu", "srs_p", "sscr"])

                    def rsq(sc):
                        return sc.activation(srs[:, :NC8], srs[:, :NC8], ACT.Sqrt)
                    S.op("act", rsq, reads=["srs_p"], writes=["srs"])
                    if which == "q":
                        def dsc_(v):
                            return v.tensor_scalar(srs[:, :NC8], srs[:, :NC8], DSC, None, ALU.mult)
                        S.op("dve", dsc_, reads=["srs"], writes=["srs"])

                    def stcopy(v, grp=grp, mubuf=mubuf, rsbuf=rsbuf):
                        v.tensor_copy(mubuf[:, grp * NC8:(grp + 1) * NC8], smu[:, :NC8])
                        return v.tensor_copy(rsbuf[:, grp * NC8:(grp + 1) * NC8], srs[:, :NC8])
                    S.op("dve", stcopy, reads=["smu", "srs"], writes=[which + "must", "ssum_free"])
                    for tcg in range(8):
                        tc = grp * 8 + tcg
                        for j in range(RH):
                            col = grp * NC8 + tcg * RH + j

                            def appl(v, tcg=tcg, j=j, col=col, mubuf=mubuf, rsbuf=rsbuf):
                                return v.tensor_scalar(ktmp[:, j * 128:(j + 1) * 128],
                                                       ygrp[:, tcg * W + j * 128: tcg * W + j * 128 + 128],
                                                       mubuf[:, col:col + 1], rsbuf[:, col:col + 1],
                                                       ALU.subtract, ALU.mult)
                            S.op("dve", appl, reads=["yg_%d_%d" % (tcg, j), which + "must"],
                                 writes=["ktmp%d" % j])
                            bank = 6 + (tcnt[0] % 2)
                            tcnt[0] += 1
                            pres = "pb%d" % bank

                            def tr(e, j=j, bank=bank):
                                return e.transpose(PBv[bank][:, 0:128], ktmp[:, j * 128:(j + 1) * 128],
                                                   ident16[:, :])
                            S.op("pe", tr, reads=["ktmp%d" % j, "ident16"], writes=[pres])

                            def cp(e, bank=bank, j=j, tc=tc, dst_list=dst_list):
                                return e.tensor_copy(dst_list[j][:, tc * 128:(tc + 1) * 128],
                                                     PBv[bank][:, 0:128])
                            S.op("dve", cp, reads=[pres], writes=["dfm_%s%d" % ("kq"[dst_list is qfm], j)])

            for r in range(NH // RH):
                proj_fm(r, "k", wk, xt, NKC, kfm, kmu, krs)
                proj_fm(r, "q", wq, xqf, NQC, qfm, qmu, qrs)
                for j in range(RH):
                    hh = r * RH + j
                    src = vsp[hh, :, :, :]
                    dst = vaug[j][:, :].rearrange("p (c f) -> p c f", f=129)[:, :, 0:128]

                    def ldv(e, src=src, dst=dst):
                        return e.dma_start(out=dst, in_=src.rearrange("c p f -> p c f"))
                    S.dma("pool", ldv, reads=["vaug_free%d" % j], writes=["vaug%d" % j])
                for j in range(RH):
                    hh = r * RH + j
                    for ci in range(NQC):
                        nkci = 2 * ci + 2
                        ngrp = (nkci + 3) // 4

                        def emit_sgrp(g):
                            k0, k1 = g * 4, min(g * 4 + 4, nkci)
                            spres = "pb%d" % (g % 2)
                            spb = PB[g % 2]
                            for kc in range(k0, k1):
                                def smm(e, j=j, kc=kc, ci=ci, spb=spb):
                                    return e.matmul(spb[:, (kc % 4) * 128:(kc % 4) * 128 + 128],
                                                    kfm[j][:, kc * 128:(kc + 1) * 128],
                                                    qfm[j][:, ci * 128:(ci + 1) * 128],
                                                    start=True, stop=True)
                                S.op("pe", smm, reads=["dfm_k%d" % j, "dfm_q%d" % j], writes=[spres])
                        emit_sgrp(0)
                        for g in range(ngrp):
                            if g + 1 < ngrp:
                                emit_sgrp(g + 1)
                            k0 = g * 4
                            k1 = min(k0 + 4, nkci)
                            w = (k1 - k0) * 128
                            spres = "pb%d" % (g % 2)
                            spb = PB[g % 2]

                            def sqev(e, spb=spb, w=w, g=g):
                                return e.activation(asb[g % 2][:, :w], spb[:, :w], ACT.Square)
                            S.op("act", sqev, reads=[spres], writes=["asb%d" % (g % 2), spres + "f"])
                            for kc in range(k0, k1):
                                if kc == nkci - 2:
                                    def mka(v, kc=kc, g=g):
                                        return v.tensor_tensor(
                                            asb[g % 2][:, (kc % 4) * 128:(kc % 4) * 128 + 128],
                                            asb[g % 2][:, (kc % 4) * 128:(kc % 4) * 128 + 128],
                                            mA_sb[:, :], ALU.mult)
                                    S.op("dve", mka, reads=["asb%d" % (g % 2), "mA_sb"],
                                         writes=["asb%d" % (g % 2)])
                                elif kc == nkci - 1:
                                    def mkb(v, kc=kc, g=g):
                                        return v.tensor_tensor(
                                            asb[g % 2][:, (kc % 4) * 128:(kc % 4) * 128 + 128],
                                            asb[g % 2][:, (kc % 4) * 128:(kc % 4) * 128 + 128],
                                            mB_sb[:, :], ALU.mult)
                                    S.op("dve", mkb, reads=["asb%d" % (g % 2), "mB_sb"],
                                         writes=["asb%d" % (g % 2)])
                            for kc in range(k0, k1):
                                def avmm(e, j=j, kc=kc, g=g, nkci=nkci):
                                    return e.matmul(PB[2][:, 0:129],
                                                    asb[g % 2][:, (kc % 4) * 128:(kc % 4) * 128 + 128],
                                                    vaug[j][:, kc * 129: kc * 129 + 129],
                                                    start=(kc == 0), stop=(kc == nkci - 1))
                                S.op("pe", avmm, reads=["asb%d" % (g % 2), "vaug%d" % j], writes=["pb2"])

                        def zev(v):
                            v.tensor_scalar(zeps[:, :], PB[2][:, 128:129], EPS, None, ALU.add)
                            v.drain()
                            return v.reciprocal(zinv[:, :], zeps[:, :])
                        S.op("dve", zev, reads=["pb2"], writes=["zinv"])

                        def onor(v):
                            return v.tensor_scalar(onorm[:, :], PB[2][:, 0:128], zinv[:, :], None, ALU.mult)
                        S.op("dve", onor, reads=["pb2", "zinv"], writes=["onorm", "pb2f"])
                        bank = 6 + (tcnt[0] % 2)
                        tcnt[0] += 1
                        pres = "pb%d" % bank

                        def otr(e, bank=bank):
                            return e.transpose(PBv[bank][:, 0:128], onorm[:, :], ident16[:, :])
                        S.op("pe", otr, reads=["onorm", "ident16"], writes=[pres])

                        def ocp(e, bank=bank, j=j, ci=ci):
                            return e.tensor_copy(ofm[j][:, ci * 128:(ci + 1) * 128], PBv[bank][:, 0:128])
                        S.op("dve", ocp, reads=[pres], writes=["ofm%d" % j])
                        dq_step(1)

                    def spo(e, j=j, hh=hh):
                        return e.dma_start(out=osp[hh, :, :], in_=ofm[j][:, :])
                    S.dma("pool", spo, reads=["ofm%d" % j], writes=["osp%d" % hh, "vaug_free%d" % j],
                          sem="osps%d" % j)
        S.fence()
    dq_step(400)
    S.fence(include_nofence=True)
    dq_es.close()

    if "wo" in phases:
        with contextlib.ExitStack() as es:
            sb = lambda n, shape, dt: gsb(es, n, shape, dt)
            wo_sb = sb("wo_sb", [128, NDMC * HID], BF16)
            for oc in range(NDMC):
                def ldwo(e, oc=oc):
                    return e.dma_start(out=wo_sb[:, oc * HID:(oc + 1) * HID],
                                       in_=wo[oc * 128:(oc + 1) * 128, :])
                S.dma("pool", ldwo, reads=[], writes=["wo_sb"])
            oq = [sb("oq%d" % i, [128, 2048], BF16) for i in range(2)]
            xts3 = [sb("xts3_%d" % i, [128, 2048], F32) for i in range(2)]
            rstage = sb("rstage", [128, 2048], F32)
            r2scr = sb("r2scr", [128, 512], F32)
            lsum = sb("lsum", [128, 4], F32)
            lsq = sb("lsq", [128, 4], F32)
            lmu = sb("lmu", [128, 1], F32)
            lrs = sb("lrs", [128, 1], F32)
            lscr = sb("lscr", [128, 1], F32)
            x1t = sb("x1t", [128, 2048], F32)
            for qc in range(NQC):
                i = qc % 2
                bank0 = (qc % 2) * 4
                src = osp[:, :, qc * 128:(qc + 1) * 128]

                def ldo(e, src=src, i=i):
                    return e.dma_start(out=oq[i][:, :].rearrange("p (h f) -> p h f", f=128),
                                       in_=src.rearrange("h p f -> p h f"))
                S.dma("sp", ldo, reads=["oqfree%d" % i], writes=["oq%d" % i])

                def ldxq(e, qc=qc, i=i):
                    return e.dma_start(out=xts3[i][:, :], in_=xqt[qc * 128:(qc + 1) * 128, :])
                S.dma("sp", ldxq, reads=["xts3free%d" % i], writes=["xts3_%d" % i])
                for dmt in range(4):
                    pres = "pb%d" % (bank0 + dmt)
                    pb = PB[bank0 + dmt]
                    for hh in range(NH):
                        def mm(e, i=i, hh=hh, pb=pb, dmt=dmt):
                            return e.matmul(pb[:, :], oq[i][:, hh * 128:(hh + 1) * 128],
                                            wo_sb[:, hh * HID + dmt * 512: hh * HID + dmt * 512 + 512],
                                            start=(hh == 0), stop=(hh == NH - 1))
                        S.op("pe", mm, reads=["oq%d" % i, "wo_sb"],
                             writes=[pres] + (["oqfree%d" % i] if (hh == NH - 1 and dmt == 3) else []))

                    def radd(e, pb=pb, dmt=dmt, i=i):
                        return e.scalar_tensor_tensor(rstage[:, dmt * 512:(dmt + 1) * 512], pb[:, :], 1.0,
                                                      xts3[i][:, dmt * 512:(dmt + 1) * 512], ALU.mult, ALU.add,
                                                      accum_out=lsum[:, dmt:dmt + 1])
                    S.op("dve", radd, reads=[pres, "xts3_%d" % i],
                         writes=["rstage%d" % dmt] + (["xts3free%d" % i] if dmt == 3 else []))

                    def rsq(e, dmt=dmt):
                        return e.activation(r2scr[:, :], rstage[:, dmt * 512:(dmt + 1) * 512], ACT.Square,
                                            accum_out=lsq[:, dmt:dmt + 1])
                    S.op("act", rsq, reads=["rstage%d" % dmt], writes=["lsq%d" % dmt])

                def lnst(v):
                    v.tensor_reduce(lmu[:, :], lsum[:, :], AX.X, ALU.add)
                    v.tensor_reduce(lscr[:, :], lsq[:, :], AX.X, ALU.add)
                    v.drain()
                    v.tensor_scalar(lmu[:, :], lmu[:, :], 1.0 / 2048.0, None, ALU.mult)
                    v.tensor_scalar(lscr[:, :], lscr[:, :], 1.0 / 2048.0, None, ALU.mult)
                    v.drain()
                    v.tensor_tensor(lrs[:, :], lmu[:, :], lmu[:, :], ALU.mult)
                    v.drain()
                    v.tensor_tensor(lscr[:, :], lscr[:, :], lrs[:, :], ALU.subtract)
                    v.drain()
                    v.tensor_scalar(lscr[:, :], lscr[:, :], EPS, None, ALU.add)
                    v.drain()
                    return v.reciprocal(lrs[:, :], lscr[:, :])
                S.op("dve", lnst, reads=["rstage%d" % d_ for d_ in range(4)]
                     + ["lsq%d" % d_ for d_ in range(4)], writes=["lmu", "lrs", "lscr"])

                def lnsq(sc):
                    return sc.activation(lrs[:, :], lrs[:, :], ACT.Sqrt)
                S.op("act", lnsq, reads=["lrs"], writes=["lrs"])

                def appl(v):
                    return v.tensor_scalar(x1t[:, :], rstage[:, :], lmu[:, :], lrs[:, :],
                                           ALU.subtract, ALU.mult)
                S.op("dve", appl, reads=["rstage%d" % d_ for d_ in range(4)] + ["lmu", "lrs"],
                     writes=["x1t"])

                def spx1(e, qc=qc):
                    return e.dma_start(out=x1s[qc, :, :], in_=x1t[:, :])
                S.dma("pool", spx1, reads=["x1t"], writes=["x1sp%d" % qc], sem="x1sp")
        S.fence()

    if "ffn" in phases:
        with contextlib.ExitStack() as es:
            sb = lambda n, shape, dt: gsb(es, n, shape, dt)
            qfs = [alloc_quant_bufs(es, nc, 512, "qf%d" % i) for i in range(2)]
            x1r = sb("x1r", [128, 2048], F32)
            x1res = x1r
            xdq_t = sb("xdq_t", [128, 2048], BF16)
            xdq = sb("xdq", [128, NDMC * 512], BF16)
            gslab = [sb("gslab%d" % i, [128, NDMC * 256], BF16) for i in range(2)]
            uslab = [sb("uslab%d" % i, [128, NDMC * 256], BF16) for i in range(2)]
            dslab = [sb("dslab%d" % i, [128, HID], BF16) for i in range(2)]
            sgbuf = [sb("sgbuf%d" % i, [128, 256], F32) for i in range(2)]
            hbuf = [sb("hbuf%d" % i, [128, 256], F32) for i in range(2)]
            hdq_t = [sb("hdq_t%d" % i, [128, 256], BF16) for i in range(2)]
            hdq = sb("hdq", [128, NIC * 512], BF16)
            f2scr = sb("f2scr", [128, 512], F32)
            fsum = sb("fsum", [128, 4], F32)
            fsq = sb("fsq", [128, 4], F32)
            fmu = sb("fmu", [128, 1], F32)
            frs = sb("frs", [128, 1], F32)
            fscr = sb("fscr", [128, 1], F32)
            outt = sb("outt", [128, 2048], F32)

            for tile in range(4):
                for qc4 in range(4):
                    qc = tile * 4 + qc4

                    def ldx1(e, qc=qc):
                        return e.dma_start(out=x1r[:, :], in_=x1s[qc, :, :])
                    S.dma("sp", ldx1, reads=["x1rfree"], writes=["x1r"])
                    for h in range(4):
                        emit_quant(S, qfs[h % 2], "qf%d" % (h % 2), x1r[:, h * 512:(h + 1) * 512], "x1r",
                                   xdq_t[:, h * 512:(h + 1) * 512], "xdq_t", 512)
                    for dmc in range(NDMC):
                        bank = 6 + (tcnt[0] % 2)
                        tcnt[0] += 1
                        pres = "pb%d" % bank

                        def tr(e, dmc=dmc, bank=bank):
                            return e.transpose(PBv[bank][:, 0:128], xdq_t[:, dmc * 128:(dmc + 1) * 128],
                                               ident16[:, :])
                        S.op("pe", tr, reads=["xdq_t", "ident16"],
                             writes=[pres] + (["x1rfree"] if dmc == NDMC - 1 else []))

                        def cp(e, bank=bank, dmc=dmc, qc4=qc4):
                            return e.tensor_copy(xdq[:, dmc * 512 + qc4 * 128: dmc * 512 + qc4 * 128 + 128],
                                                 PBv[bank][:, 0:128])
                        S.op("dve", cp, reads=[pres], writes=["xdq"])
                for ig in range(32):
                    sl = ig % 2
                    for nm, slab, wspl in (("g", gslab, gws), ("u", uslab, uws)):
                        src = wspl[:, :, ig * 256:(ig + 1) * 256]
                        for piece in range(4):
                            def ldsl(e, src=src, slab=slab, piece=piece, sl=sl):
                                return e.dma_start(
                                    out=slab[sl][:, piece * 4 * 256:(piece + 1) * 4 * 256].rearrange(
                                        "p (c f) -> p c f", f=256),
                                    in_=src[piece * 4:(piece + 1) * 4].rearrange("c p f -> p c f"))
                            S.dma("sp", ldsl, reads=[nm + "slabfree%d" % sl], writes=[nm + "slab%d" % sl])
                    for ph in range(2):
                        for q2 in range(2):
                            qc4 = 2 * ph + q2
                            presg = "pb%d" % (q2 * 2)
                            presu = "pb%d" % (q2 * 2 + 1)
                            pbg = PB[q2 * 2]
                            pbu = PB[q2 * 2 + 1]
                            for dmc in range(NDMC):
                                def mmg(e, dmc=dmc, qc4=qc4, pbg=pbg, pbu=pbu, sl=sl):
                                    e.matmul(pbg[:, 0:256],
                                             xdq[:, dmc * 512 + qc4 * 128: dmc * 512 + qc4 * 128 + 128],
                                             gslab[sl][:, dmc * 256:(dmc + 1) * 256],
                                             start=(dmc == 0), stop=(dmc == NDMC - 1))
                                    return e.matmul(pbu[:, 0:256],
                                                    xdq[:, dmc * 512 + qc4 * 128: dmc * 512 + qc4 * 128 + 128],
                                                    uslab[sl][:, dmc * 256:(dmc + 1) * 256],
                                                    start=(dmc == 0), stop=(dmc == NDMC - 1))
                                S.op("pe", mmg,
                                     reads=["xdq", "gslab%d" % sl, "uslab%d" % sl],
                                     writes=[presg, presu] + (["gslabfree%d" % sl, "uslabfree%d" % sl]
                                                              if (dmc == NDMC - 1 and ph == 1 and q2 == 1)
                                                              else []))
                        for q2 in range(2):
                            qc4 = 2 * ph + q2
                            presg = "pb%d" % (q2 * 2)
                            presu = "pb%d" % (q2 * 2 + 1)
                            pbg = PB[q2 * 2]
                            pbu = PB[q2 * 2 + 1]

                            def silu(e, pbg=pbg, q2=q2):
                                return e.activation(sgbuf[q2][:, :], pbg[:, 0:256], ACT.Silu)
                            S.op("act", silu, reads=[presg], writes=["sgbuf%d" % q2, presg + "f"])

                            def hmul(v, pbu=pbu, q2=q2):
                                return v.tensor_tensor(hbuf[q2][:, :], sgbuf[q2][:, :],
                                                       pbu[:, 0:256], ALU.mult)
                            S.op("dve", hmul, reads=["sgbuf%d" % q2, presu],
                                 writes=["hbuf%d" % q2, presu + "f"])
                        emit_quant_multi(S, [
                            (qfs[q2], "qf%d" % q2, hbuf[q2][:, :256], "hbuf%d" % q2,
                             hdq_t[q2][:, :256], "hdq_t%d" % q2, 256)
                            for q2 in range(2)])
                        for q2 in range(2):
                            qc4 = 2 * ph + q2
                            for t in range(2):
                                ic = ig * 2 + t
                                bank = 6 + (tcnt[0] % 2)
                                tcnt[0] += 1
                                pres = "pb%d" % bank

                                def tr(e, t=t, bank=bank, q2=q2):
                                    return e.transpose(PBv[bank][:, 0:128],
                                                       hdq_t[q2][:, t * 128:(t + 1) * 128],
                                                       ident16[:, :])
                                S.op("pe", tr, reads=["hdq_t%d" % q2, "ident16"], writes=[pres])

                                def cp(e, bank=bank, ic=ic, qc4=qc4):
                                    return e.tensor_copy(
                                        hdq[:, ic * 512 + qc4 * 128: ic * 512 + qc4 * 128 + 128],
                                        PBv[bank][:, 0:128])
                                S.op("dve", cp, reads=[pres], writes=["hdq"])
                for half in range(2):
                    for ic in range(NIC):
                        i = ic % 2

                        def lddw(e, ic=ic, i=i):
                            return e.dma_start(out=dslab[i][:, :], in_=dws[ic, :, :])
                        S.dma("sp", lddw, reads=["dslabfree%d" % i], writes=["dslab%d" % i])
                        for q2 in range(2):
                            qc4 = half * 2 + q2

                            def mmd(e, ic=ic, i=i, qc4=qc4, q2=q2):
                                last = None
                                for dmt in range(4):
                                    last = e.matmul(
                                        PB[q2 * 4 + dmt][:, :],
                                        hdq[:, ic * 512 + qc4 * 128: ic * 512 + qc4 * 128 + 128],
                                        dslab[i][:, dmt * 512:(dmt + 1) * 512],
                                        start=(ic == 0), stop=(ic == NIC - 1))
                                return last
                            S.op("pe", mmd, reads=["hdq", "dslab%d" % i],
                                 writes=["pbd%d" % q2] + (["dslabfree%d" % i] if q2 == 1 else []))
                    for q2 in range(2):
                        qc4 = half * 2 + q2
                        qc = tile * 4 + qc4

                        def ldx1b(e, qc=qc):
                            return e.dma_start(out=x1res[:, :], in_=x1s[qc, :, :])
                        S.dma("sp", ldx1b, reads=["x1rfree"], writes=["x1r"])
                        for dmt in range(4):
                            pb = PB[q2 * 4 + dmt]

                            def radd(e, pb=pb, dmt=dmt):
                                return e.scalar_tensor_tensor(
                                    outt[:, dmt * 512:(dmt + 1) * 512], pb[:, :], 1.0,
                                    x1res[:, dmt * 512:(dmt + 1) * 512], ALU.mult, ALU.add,
                                    accum_out=fsum[:, dmt:dmt + 1])
                            S.op("dve", radd, reads=["pbd%d" % q2, "x1r"],
                                 writes=["outt%d" % dmt] + (["x1rfree"] if dmt == 3 else []))

                            def rsq(e, dmt=dmt):
                                return e.activation(f2scr[:, :], outt[:, dmt * 512:(dmt + 1) * 512],
                                                    ACT.Square, accum_out=fsq[:, dmt:dmt + 1])
                            S.op("act", rsq, reads=["outt%d" % dmt], writes=["fsq%d" % dmt])

                        def lnst(v):
                            v.tensor_reduce(fmu[:, :], fsum[:, :], AX.X, ALU.add)
                            v.tensor_reduce(fscr[:, :], fsq[:, :], AX.X, ALU.add)
                            v.drain()
                            v.tensor_scalar(fmu[:, :], fmu[:, :], 1.0 / 2048.0, None, ALU.mult)
                            v.tensor_scalar(fscr[:, :], fscr[:, :], 1.0 / 2048.0, None, ALU.mult)
                            v.drain()
                            v.tensor_tensor(frs[:, :], fmu[:, :], fmu[:, :], ALU.mult)
                            v.drain()
                            v.tensor_tensor(fscr[:, :], fscr[:, :], frs[:, :], ALU.subtract)
                            v.drain()
                            v.tensor_scalar(fscr[:, :], fscr[:, :], EPS, None, ALU.add)
                            v.drain()
                            return v.reciprocal(frs[:, :], fscr[:, :])
                        S.op("dve", lnst, reads=["outt%d" % d_ for d_ in range(4)]
                             + ["fsq%d" % d_ for d_ in range(4)], writes=["fmu", "frs", "fscr"])

                        def lnsq(sc):
                            return sc.activation(frs[:, :], frs[:, :], ACT.Sqrt)
                        S.op("act", lnsq, reads=["frs"], writes=["frs"])

                        def appl(v):
                            return v.tensor_scalar(outt[:, :], outt[:, :], fmu[:, :], frs[:, :],
                                                   ALU.subtract, ALU.mult)
                        S.op("dve", appl, reads=["outt%d" % d_ for d_ in range(4)] + ["fmu", "frs"],
                             writes=["outt0", "outt1", "outt2", "outt3"])

                        def spout(e, qc=qc):
                            return e.dma_start(out=out[qc * 128:(qc + 1) * 128, :], in_=outt[:, :])
                        S.dma("pool", spout, reads=["outt%d" % d_ for d_ in range(4)],
                              writes=["outd%d" % qc], sem="outsp")
        S.fence()

    o = S.op("sp", None, reads=[], writes=[])
    o.waits = S.final_waits()
    o.deps = set()

    S.emit(nc, root)
    root.close()
    return nc


def core_inputs(core, x, Wq, Wk, Wv, Wo, gamma, beta, gate_w, up_w, down_w):
    b = core // 2
    half = core % 2
    xb = np.ascontiguousarray(x[b])
    xtb = np.ascontiguousarray(xb.T)
    qcols = np.concatenate([np.arange(c * 128, (c + 1) * 128)
                            for c in range(half, 32, 2)])
    tri = np.triu(np.ones((128, 128), dtype=np.float32))
    if half == 0:
        mA, mB = tri, np.zeros((128, 128), np.float32)
    else:
        mA, mB = np.ones((128, 128), np.float32), tri
    d = {
        "xt": xtb,
        "xqf": np.ascontiguousarray(xtb[:, qcols]),
        "xqt": np.ascontiguousarray(xb[qcols]),
        "wqT": np.ascontiguousarray(Wq.T),
        "wkT": np.ascontiguousarray(Wk.T),
        "wvT": np.ascontiguousarray(Wv.T),
        "woT": np.ascontiguousarray(Wo.T),
        "gw": np.ascontiguousarray(gate_w),
        "uw": np.ascontiguousarray(up_w),
        "dw": np.ascontiguousarray(down_w),
        "gb2": np.ascontiguousarray(np.broadcast_to(np.tile(gamma, 4)[None, :], (128, 512))),
        "bb2": np.ascontiguousarray(np.broadcast_to(np.tile(beta, 4)[None, :], (128, 512))),
        "mA": mA,
        "mB": mB,
        "idin": np.eye(128, dtype=np.float32),
    }
    return d, qcols


_NC_CACHE = {}


def _get_nc():
    if "nc" not in _NC_CACHE:
        _NC_CACHE["nc"] = build_program({})
    return _NC_CACHE["nc"]


def kernel(L0_attn_hidden_states, L0_attn_Wq, L0_attn_Wk, L0_attn_Wv, L0_attn_Wo,
           L0_attn_gamma, L0_attn_beta, L0_ffn_gate_w, L0_ffn_up_w, L0_ffn_down_w):
    from concourse.bass_utils import run_bass_kernel_spmd
    x = np.asarray(L0_attn_hidden_states, dtype=np.float32)
    args = (x, np.asarray(L0_attn_Wq, np.float32), np.asarray(L0_attn_Wk, np.float32),
            np.asarray(L0_attn_Wv, np.float32), np.asarray(L0_attn_Wo, np.float32),
            np.asarray(L0_attn_gamma, np.float32), np.asarray(L0_attn_beta, np.float32),
            np.asarray(L0_ffn_gate_w, np.float32), np.asarray(L0_ffn_up_w, np.float32),
            np.asarray(L0_ffn_down_w, np.float32))
    nc = _get_nc()
    ims = []
    qcls = []
    for c in range(8):
        d, qc = core_inputs(c, *args)
        ims.append(d)
        qcls.append(qc)
    res = run_bass_kernel_spmd(nc, ims, core_ids=list(range(8)))
    out = np.empty((4, 4096, 2048), dtype=np.float32)
    for c in range(8):
        out[c // 2][qcls[c]] = np.asarray(res.results[c]["outp"], dtype=np.float32)
    return out


# version 05
# speedup vs baseline: 1.0705x; 1.0705x over previous
import contextlib
import numpy as np
import concourse.bass as bass
import concourse.mybir as mybir

F32 = mybir.dt.float32
I32 = mybir.dt.int32
BF16 = mybir.dt.bfloat16
ALU = mybir.AluOpType
ACT = mybir.ActivationFunctionType
AX = mybir.AxisListType

HID = 2048
TKV = 4096
NH = 16
DH = 128
II = 8192
NQC = 16
NKC = 32
NDMC = 16
NIC = 64
EPS = 1e-5
DSC = float(DH) ** -0.5

ENGINES = ("pe", "act", "dve", "pool", "sp")


class Op:
    __slots__ = ("eng", "fn", "reads", "writes", "dma", "sem_grp", "bulk", "idx",
                 "waits", "need_inc", "deps", "inc_val", "same_max", "nofence")

    def __init__(self, eng, fn, reads, writes, dma, sem_grp=None, bulk=False):
        self.eng = eng
        self.fn = fn
        self.reads = list(reads)
        self.writes = list(writes)
        self.dma = dma
        self.sem_grp = sem_grp
        self.bulk = bulk
        self.idx = None
        self.waits = []
        self.need_inc = False
        self.deps = set()
        self.inc_val = 0
        self.same_max = 0
        self.nofence = False


class Sched:

    def __init__(self):
        self.ops = []
        self.last_w = {}
        self.readers = {}
        self.ngrp = {}
        self.neng = {e: 0 for e in ENGINES}
        self.fence_snapshot = []
        self.last_op = {e: None for e in ENGINES}
        self.nofence_groups = set()

    def op(self, eng, fn, reads=(), writes=(), dma=None, sem_grp=None, bulk=False,
           nofence=False):
        o = Op(eng, fn, reads, writes, dma, sem_grp, bulk)
        o.nofence = nofence
        if nofence and dma is not None:
            self.nofence_groups.add(sem_grp)
        if dma is not None:
            self.ngrp[sem_grp] = self.ngrp.get(sem_grp, 0) + 1
            o.idx = self.ngrp[sem_grp]
        else:
            self.neng[eng] += 1
            o.idx = self.neng[eng]
        deps = set()
        for r in o.reads:
            w = self.last_w.get(r)
            if w is not None:
                deps.add(w)
        for r in o.writes:
            w = self.last_w.get(r)
            if w is not None:
                deps.add(w)
            for rd in self.readers.get(r, ()):
                deps.add(rd)
        if not nofence:
            for fop in self.fence_snapshot:
                deps.add(fop)
        o.deps = deps
        self.ops.append(o)
        for r in o.writes:
            self.last_w[r] = o
            self.readers[r] = []
        for r in o.reads:
            self.readers.setdefault(r, []).append(o)
        if dma is None:
            self.last_op[eng] = o
        return o

    def dma(self, eng, fn, reads=(), writes=(), sem=None, bulk=False, nofence=False):
        grp = sem if sem is not None else (writes[0] if writes else "misc")
        return self.op(eng, fn, reads, writes, dma=eng, sem_grp=grp, bulk=bulk,
                       nofence=nofence)

    def fence(self, include_nofence=False):
        snap = []
        for e in ENGINES:
            if self.last_op[e] is not None:
                snap.append(self.last_op[e])
        seen = set()
        for o in reversed(self.ops):
            if o.dma is not None and o.sem_grp not in seen:
                if o.sem_grp in self.nofence_groups and not include_nofence:
                    continue
                seen.add(o.sem_grp)
                snap.append(o)
        self.fence_snapshot = snap
        for o in snap:
            if o.dma is not None:
                o.bulk = True

    def resolve(self):
        totals = dict(self.ngrp)
        for o in self.ops:
            for d in o.deps:
                if d is o:
                    continue
                if d.dma is not None:
                    v = 16 * (totals[d.sem_grp] if d.bulk else d.idx)
                    o.waits.append((("g", d.sem_grp), v))
                elif d.eng != o.eng:
                    d.need_inc = True
                    o.waits.append((("eng", d.eng), d))
                else:
                    if o.eng in ("dve", "act"):
                        d.need_inc = True
                        o.waits.append((("eng", d.eng), d))
        cnt = {e: 0 for e in ENGINES}
        for o in self.ops:
            if o.dma is None and o.need_inc:
                cnt[o.eng] += 1
            o.inc_val = cnt[o.eng]
        for o in self.ops:
            o.waits = [(k, (v if isinstance(v, int) else v.inc_val)) for k, v in o.waits]

    def final_waits(self):
        return [(("g", g), 16 * n) for g, n in self.ngrp.items()]

    def emit(self, nc, es):
        self.resolve()
        sems = {("eng", e): es.enter_context(nc.semaphore("sem_" + e)) for e in ENGINES}
        for g in self.ngrp:
            sems[("g", g)] = es.enter_context(nc.semaphore("sg_" + g))
        per_eng = {e: [] for e in ENGINES}
        for o in self.ops:
            per_eng[o.eng].append(o)
        block = es.enter_context(nc.Block())

        def mk(eng_name):
            ops = per_eng[eng_name]

            def prog(eng):
                hi = {}
                for o in ops:
                    for k, v in o.waits:
                        if v > hi.get(k, 0):
                            hi[k] = v
                            eng.wait_ge(sems[k], v)
                    if o.fn is None:
                        continue
                    inst = o.fn(eng)
                    if o.dma is not None:
                        inst.then_inc(sems[("g", o.sem_grp)], 16)
                    elif o.need_inc:
                        inst.then_inc(sems[("eng", o.eng)], 1)
            return prog

        block.tensor(mk("pe"))
        block.scalar(mk("act"))
        block.vector(mk("dve"))
        block.gpsimd(mk("pool"))
        block.sync(mk("sp"))


def quant_stages(b, tag, src_ap, src_res, dq_ap, dq_res, C):
    nb = C // 32
    B = lambda n: b[n][:, :C]
    Bb = lambda n: b[n][:, :nb]
    rs = lambda *names: [tag + n for n in names]

    def st_red(v):
        s3 = src_ap.rearrange("p (b w) -> p b w", w=32)
        return v.tensor_reduce(Bb("m"), s3, AX.X, ALU.max, apply_absolute_value=True)

    def st_s(v):
        return v.tensor_scalar(Bb("s"), Bb("m"), 1.0 / 448.0, 1e-30, ALU.mult, ALU.max)

    def st_si(v):
        return v.reciprocal(Bb("sinv"), Bb("s"))

    def st_y(v):
        s3 = src_ap.rearrange("p (b w) -> p b w", w=32)
        sb = Bb("sinv").unsqueeze(2).broadcast_to([128, nb, 32])
        return v.tensor_tensor(B("y").rearrange("p (b w) -> p b w", w=32), s3, sb, ALU.mult)

    def st_ei(v):
        return v.tensor_scalar(B("ei"), B("y").bitcast(I32), 23, 255,
                               ALU.logical_shift_right, ALU.bitwise_and)

    def st_ub(v):
        return v.tensor_scalar(B("ub"), B("ei"), 121.0, 8388608.0, ALU.max, ALU.mult)

    def st_ui(sc):
        return sc.activation(B("ui"), B("ub"), ACT.Copy, bias=2155872256.0, scale=-1.0)

    def st_ri(v):
        return v.tensor_tensor(B("ri"), B("y"), B("ui").bitcast(F32), ALU.mult)

    def st_rf(sc):
        return sc.activation(B("rf"), B("ri"), ACT.Copy, bias=0.0, scale=0.125)

    def st_q(v):
        return v.tensor_tensor(B("q"), B("rf"), B("ub").bitcast(F32), ALU.mult)

    def st_dq(v):
        sb2 = Bb("s").unsqueeze(2).broadcast_to([128, nb, 32])
        return v.tensor_tensor(dq_ap.rearrange("p (b w) -> p b w", w=32),
                               B("q").rearrange("p (b w) -> p b w", w=32), sb2, ALU.mult)

    return [
        ("dve", st_red, [src_res], rs("m")),
        ("dve", st_s, rs("m"), rs("s")),
        ("dve", st_si, rs("s"), rs("sinv")),
        ("dve", st_y, [src_res] + rs("sinv"), rs("y")),
        ("dve", st_ei, rs("y"), rs("ei")),
        ("dve", st_ub, rs("ei"), rs("ub")),
        ("act", st_ui, rs("ub"), rs("ui")),
        ("dve", st_ri, rs("y", "ui"), rs("ri")),
        ("act", st_rf, rs("ri"), rs("rf")),
        ("dve", st_q, rs("rf", "ub"), rs("q")),
        ("dve", st_dq, rs("q", "s"), [dq_res]),
    ]


def emit_quant_multi(S, jobs, nofence=False):
    stages = [quant_stages(*j) for j in jobs]
    for si in range(11):
        for st in stages:
            eng, fn, rd, wr = st[si]
            S.op(eng, fn, reads=rd, writes=wr, nofence=nofence)


def emit_quant(S, b, tag, src_ap, src_res, dq_ap, dq_res, C):
    emit_quant_multi(S, [(b, tag, src_ap, src_res, dq_ap, dq_res, C)])


def alloc_quant_bufs(es, nc, C, tag):
    sb = lambda n, dt, w=C: es.enter_context(nc.sbuf_tensor(tag + n, [128, w], dt))
    b = {}
    b["m"] = sb("m", F32, C // 32)
    b["s"] = sb("s", F32, C // 32)
    b["sinv"] = sb("si", F32, C // 32)
    b["y"] = sb("y", F32)
    b["ei"] = sb("ei", I32)
    b["ub"] = sb("ub", I32)
    b["ui"] = sb("ui", I32)
    b["ri"] = sb("ri", I32)
    b["rf"] = sb("rf", F32)
    b["q"] = sb("q", F32)
    return b


def build_program(cfg=None):
    cfg = cfg or {}
    nc = bass.Bass("TRN2", target_bir_lowering=False)
    root = contextlib.ExitStack()
    S = Sched()
    phases = cfg.get("phases", ("dequant", "v", "attn", "wo", "ffn"))
    tap = cfg.get("tap", None)

    xt = nc.dram_tensor("xt", [HID, TKV], F32, kind="ExternalInput")
    xqf = nc.dram_tensor("xqf", [HID, 2048], F32, kind="ExternalInput")
    xqt = nc.dram_tensor("xqt", [2048, HID], F32, kind="ExternalInput")
    wq = nc.dram_tensor("wqT", [HID, HID], F32, kind="ExternalInput")
    wk = nc.dram_tensor("wkT", [HID, HID], F32, kind="ExternalInput")
    wv = nc.dram_tensor("wvT", [HID, HID], F32, kind="ExternalInput")
    wo = nc.dram_tensor("woT", [HID, HID], F32, kind="ExternalInput")
    gwd = nc.dram_tensor("gw", [II, HID], F32, kind="ExternalInput")
    uwd = nc.dram_tensor("uw", [II, HID], F32, kind="ExternalInput")
    dwd = nc.dram_tensor("dw", [HID, II], F32, kind="ExternalInput")
    gb2 = nc.dram_tensor("gb2", [128, 512], F32, kind="ExternalInput")
    bb2 = nc.dram_tensor("bb2", [128, 512], F32, kind="ExternalInput")
    mAd = nc.dram_tensor("mA", [128, 128], F32, kind="ExternalInput")
    mBd = nc.dram_tensor("mB", [128, 128], F32, kind="ExternalInput")
    idin = nc.dram_tensor("idin", [128, 128], F32, kind="ExternalInput")
    out = nc.dram_tensor("outp", [2048, HID], F32, kind="ExternalOutput")

    ik = lambda t: "ExternalOutput" if tap == t else "Internal"
    gws = nc.dram_tensor("gws", [NDMC, 128, II], BF16, kind=ik("dq"))
    uws = nc.dram_tensor("uws", [NDMC, 128, II], BF16, kind=ik("dq"))
    dws = nc.dram_tensor("dws", [NIC, 128, HID], BF16, kind=ik("dq"))
    vsp = nc.dram_tensor("vsp", [NH, NKC, 128, 128], BF16, kind=ik("v"))
    osp = nc.dram_tensor("osp", [NH, 128, 2048], BF16,
                         kind=("ExternalOutput" if tap in ("o", "azo") else "Internal"))
    x1s = nc.dram_tensor("x1s", [NQC, 128, HID], F32, kind=ik("x1"))
    kqd = nc.dram_tensor("kqd", [8, 128, TKV], BF16, kind=ik("kq"))
    azd = nc.dram_tensor("azd", [4, 128, 512], F32,
                         kind=("ExternalOutput" if tap in ("az", "azo") else "Internal"))

    gsb = lambda es_, n, shape, dt: es_.enter_context(nc.sbuf_tensor(n, shape, dt))
    PB = [root.enter_context(nc.psum_tensor("pb%d" % i, [128, 512], F32)) for i in range(8)]
    PBv = [PB[i][:, :].bitcast(BF16) for i in range(8)]

    ident16 = gsb(root, "ident16", [128, 128], BF16)
    S.dma("pool", lambda e: e.dma_start(out=ident16[:, :], in_=idin[:, :]),
          reads=[], writes=["ident16"])

    tcnt = [0]

    def transpose_to(src_ap_list, src_res, dst_ap_of, dst_res):
        n = len(src_ap_list)
        for base in range(0, n, 4):
            m = min(4, n - base)
            bank = 6 + (tcnt[0] % 2)
            tcnt[0] += 1
            pres = "pb%d" % bank
            for t in range(m):
                def tr(e, t=t, base=base, bank=bank):
                    return e.transpose(PBv[bank][:, (t * 128):(t * 128 + 128)],
                                       src_ap_list[base + t], ident16[:, :])
                S.op("pe", tr, reads=[src_res, "ident16"],
                     writes=[pres] if t == m - 1 else [pres])

            def cp(e, base=base, m=m, bank=bank):
                return e.activation(dst_ap_of(base, m), PBv[bank][:, 0:m * 128], ACT.Copy)
            S.op("act", cp, reads=[pres], writes=[dst_res])

    dq_gen_obj = None
    dq_es = contextlib.ExitStack()
    if "dequant" in phases:
        sbd = lambda n, shape, dt: gsb(dq_es, n, shape, dt)
        qbs = [alloc_quant_bufs(dq_es, nc, 1024, "qd%d" % i) for i in range(2)]
        raw = [sbd("rawd%d" % i, [128, 2048], F32) for i in range(2)]
        dqc = [sbd("dqcb%d" % i, [128, 1024], BF16) for i in range(2)]
        band = sbd("band", [128, 16 * 512], BF16)

        def dq_gen():
            for wname, win, wsp, nrc, nslab in (("gw", gwd, gws, 64, 1), ("uw", uwd, uws, 64, 1),
                                                ("dw", dwd, dws, 16, 4)):
                for slab in range(nslab):
                    for bandi in range(nrc // 4):
                        for rc4 in range(4):
                            rc = bandi * 4 + rc4
                            i = rc % 2
                            src = win[rc * 128:(rc + 1) * 128, slab * 2048:(slab + 1) * 2048]

                            def ld(e, src=src, i=i):
                                return e.dma_start(out=raw[i][:, :], in_=src)
                            S.dma("sp", ld, reads=[], writes=["rawd%d" % i], nofence=True)
                            emit_quant_multi(S, [
                                (qbs[h], "qd%d" % h, raw[i][:, h * 1024:(h + 1) * 1024],
                                 "rawd%d" % i, dqc[h][:, :], "dqcq%d" % h, 1024)
                                for h in range(2)], nofence=True)
                            for h in range(2):
                                for t in range(8):
                                    dchunk = h * 8 + t
                                    bank = 6 + (tcnt[0] % 2)
                                    tcnt[0] += 1
                                    pres = "pb%d" % bank

                                    def tr(e, h=h, t=t, bank=bank):
                                        return e.transpose(PBv[bank][:, 0:128],
                                                           dqc[h][:, t * 128:(t + 1) * 128],
                                                           ident16[:, :])
                                    S.op("pe", tr, reads=["dqcq%d" % h, "ident16"], writes=[pres],
                                         nofence=True)

                                    def cp(e, bank=bank, dchunk=dchunk, rc4=rc4):
                                        return e.activation(
                                            band[:, dchunk * 512 + rc4 * 128: dchunk * 512 + rc4 * 128 + 128],
                                            PBv[bank][:, 0:128], ACT.Copy)
                                    S.op("act", cp, reads=[pres], writes=["band_c%d" % dchunk],
                                         nofence=True)
                            yield
                        for dchunk in range(16):
                            dst = wsp[slab * 16 + dchunk, :, bandi * 512:(bandi + 1) * 512]

                            def sp_(e, dst=dst, dchunk=dchunk):
                                return e.dma_start(out=dst, in_=band[:, dchunk * 512: (dchunk + 1) * 512])
                            S.dma("pool", sp_, reads=["band_c%d" % dchunk],
                                  writes=["%s_sp%d_%d" % (wname, slab * 16 + dchunk, bandi)],
                                  sem="wsps%d" % dchunk, nofence=True)
                        yield
        dq_gen_obj = dq_gen()

    def dq_step(n=1):
        if dq_gen_obj is not None:
            for _ in range(n):
                if next(dq_gen_obj, "END") == "END":
                    break

    if "v" in phases:
        with contextlib.ExitStack() as es:
            sb = lambda n, shape, dt: gsb(es, n, shape, dt)
            wv_sb = sb("wv_sb", [128, NDMC * HID], BF16)
            for dmc in range(NDMC):
                def ldwv(e, dmc=dmc):
                    return e.dma_start(out=wv_sb[:, dmc * HID:(dmc + 1) * HID],
                                       in_=wv[dmc * 128:(dmc + 1) * 128, :])
                S.dma("pool", ldwv, reads=[], writes=["wv_sb"])
            xt_sb = [sb("xt_sb%d" % i, [128, NDMC * 128], BF16) for i in range(2)]
            vstage = [sb("vstage%d" % i, [128, 512], BF16) for i in range(4)]
            for tc in range(NKC):
                i = tc % 2
                src = xt[:, tc * 128:(tc + 1) * 128].rearrange("(c p) f -> p c f", p=128)

                def ldxt(e, src=src, i=i):
                    return e.dma_start(out=xt_sb[i][:, :].rearrange("p (c f) -> p c f", f=128), in_=src)
                S.dma("pool", ldxt, reads=[], writes=["xts%d" % i])
                for hg in range(4):
                    pres = "pb%d" % hg
                    pb = PB[hg]
                    for dmc in range(NDMC):
                        def mm(e, i=i, dmc=dmc, hg=hg, pb=pb):
                            return e.matmul(pb[:, :], xt_sb[i][:, dmc * 128:(dmc + 1) * 128],
                                            wv_sb[:, dmc * HID + hg * 512: dmc * HID + hg * 512 + 512],
                                            start=(dmc == 0), stop=(dmc == NDMC - 1))
                        S.op("pe", mm, reads=["xts%d" % i, "wv_sb"], writes=[pres])
                    vst = (tc % 2) * 2 + hg % 2

                    def ev(e, pb=pb, vst=vst):
                        return e.activation(vstage[vst][:, :], pb[:, :], ACT.Copy)
                    S.op("act", ev, reads=[pres], writes=["vstage%d" % vst])
                    dst = vsp[hg * 4: hg * 4 + 4, tc, :, :]

                    def spv(e, dst=dst, vst=vst):
                        return e.dma_start(out=dst.rearrange("h p f -> p h f"),
                                           in_=vstage[vst][:, :].rearrange("p (h f) -> p h f", f=128))
                    S.dma("pool", spv, reads=["vstage%d" % vst], writes=["vsp_%d_%d" % (tc, hg)],
                          sem="vsps%d" % vst)
                dq_step(1)
        S.fence()

    RH = 2
    if "attn" in phases:
        with contextlib.ExitStack() as es:
            sb = lambda n, shape, dt: gsb(es, n, shape, dt)
            gb_sb = sb("gb_sb", [128, 512], F32)
            bb_sb = sb("bb_sb", [128, 512], F32)
            mA_sb = sb("mA_sb", [128, 128], F32)
            mB_sb = sb("mB_sb", [128, 128], F32)
            S.dma("sp", lambda e: e.dma_start(out=gb_sb[:, :], in_=gb2[:, :]), reads=[], writes=["gb_sb"])
            S.dma("sp", lambda e: e.dma_start(out=bb_sb[:, :], in_=bb2[:, :]), reads=[], writes=["bb_sb"])
            S.dma("sp", lambda e: e.dma_start(out=mA_sb[:, :], in_=mAd[:, :]), reads=[], writes=["mA_sb"])
            S.dma("sp", lambda e: e.dma_start(out=mB_sb[:, :], in_=mBd[:, :]), reads=[], writes=["mB_sb"])

            kfm = [sb("kfm%d" % j, [128, TKV], BF16) for j in range(RH)]
            qfm = [sb("qfm%d" % j, [128, 2048], BF16) for j in range(RH)]
            vaug = [sb("vaug%d" % j, [128, NKC * 129], BF16) for j in range(RH)]
            for j in range(RH):
                def ms(e, j=j):
                    return e.memset(vaug[j][:, :], 1.0)
                S.op("dve", ms, reads=[], writes=["vaug%d" % j])
            wfsb = {"k": sb("wfsb_k", [128, NDMC * 128 * RH], BF16),
                    "q": sb("wfsb_q", [128, NDMC * 128 * RH], BF16)}
            wslc = [sb("wslc%d" % i, [128, 128 * RH], BF16) for i in range(2)]
            xts2 = [sb("xts2_%d" % i, [128, NDMC * 128], BF16) for i in range(2)]
            ygrp = sb("ygrp", [128, 8 * 128 * RH], F32)
            y2scr = sb("y2scr", [128, 128 * RH], F32)
            ktmp = sb("ktmp", [128, 128 * RH], BF16)
            ssum = sb("ssum", [128, 8 * RH], F32)
            ssq = sb("ssq", [128, 8 * RH], F32)
            smu = sb("smu", [128, 8 * RH], F32)
            srs = sb("srs", [128, 8 * RH], F32)
            sscr = sb("sscr", [128, 8 * RH], F32)
            kmu = sb("kmu", [128, NKC * RH], F32)
            krs = sb("krs", [128, NKC * RH], F32)
            qmu = sb("qmu", [128, NQC * RH], F32)
            qrs = sb("qrs", [128, NQC * RH], F32)
            asb = [sb("asb%d" % i, [128, 512], BF16) for i in range(2)]
            onorm = sb("onorm", [128, 128], BF16)
            ofm = [sb("ofm%d" % j, [128, 2048], BF16) for j in range(RH)]
            zeps = sb("zeps", [128, 1], F32)
            zinv = sb("zinv", [128, 1], F32)
            W = 128 * RH

            def proj_fm(r, which, wdram, src_dram, nchunks, dst_list, mubuf, rsbuf):
                for dmc in range(NDMC):
                    i = dmc % 2

                    def ldw(e, dmc=dmc, i=i, wdram=wdram, r=r):
                        return e.dma_start(out=wslc[i][:, :],
                                           in_=wdram[dmc * 128:(dmc + 1) * 128, r * W:(r + 1) * W])
                    S.dma("pool", ldw, reads=["wslc_free%d" % i], writes=["wslc%d" % i])

                    def fold(v, dmc=dmc, i=i, which=which):
                        return v.tensor_tensor(wfsb[which][:, dmc * W:(dmc + 1) * W],
                                               wslc[i][:, :], gb_sb[:, :W], ALU.mult)
                    S.op("dve", fold, reads=["wslc%d" % i, "gb_sb"],
                         writes=["wfsb_" + which, "wslc_free%d" % i])
                ngrp = nchunks // 8
                for grp in range(ngrp):
                    for tcg in range(8):
                        tc = grp * 8 + tcg
                        i = tc % 2
                        ssrc = src_dram[:, tc * 128:(tc + 1) * 128].rearrange("(c p) f -> p c f", p=128)

                        def ldx(e, ssrc=ssrc, i=i):
                            return e.dma_start(out=xts2[i][:, :].rearrange("p (c f) -> p c f", f=128),
                                               in_=ssrc)
                        S.dma("pool", ldx, reads=["xts2free%d" % i], writes=["xts2_%d" % i])
                        pres = "pb%d" % (4 + (tc % 3) if (tc % 3) < 2 else 3)
                        pres = ["pb4", "pb5", "pb3"][tc % 3]
                        pb = PB[[4, 5, 3][tc % 3]]
                        for dmc in range(NDMC):
                            def mm(e, i=i, dmc=dmc, pb=pb, which=which):
                                return e.matmul(pb[:, 0:W], xts2[i][:, dmc * 128:(dmc + 1) * 128],
                                                wfsb[which][:, dmc * W:(dmc + 1) * W],
                                                start=(dmc == 0), stop=(dmc == NDMC - 1))
                            S.op("pe", mm, reads=["xts2_%d" % i, "wfsb_" + which],
                                 writes=[pres, "xts2free%d" % i])
                        for j in range(RH):
                            col = tcg * RH + j

                            def badd(e, pb=pb, j=j, tcg=tcg, col=col):
                                return e.scalar_tensor_tensor(
                                    ygrp[:, tcg * W + j * 128: tcg * W + j * 128 + 128],
                                    pb[:, j * 128:(j + 1) * 128], 1.0,
                                    bb_sb[:, j * 128:(j + 1) * 128], ALU.mult, ALU.add,
                                    accum_out=ssum[:, col:col + 1])
                            S.op("dve", badd, reads=[pres, "bb_sb"], writes=["yg_%d_%d" % (tcg, j)])

                            def sqa(e, tcg=tcg, j=j, col=col):
                                return e.activation(y2scr[:, j * 128:(j + 1) * 128],
                                                    ygrp[:, tcg * W + j * 128: tcg * W + j * 128 + 128],
                                                    ACT.Square, accum_out=ssq[:, col:col + 1])
                            S.op("act", sqa, reads=["yg_%d_%d" % (tcg, j)], writes=["sq_%d_%d" % (tcg, j)])
                    ygres = ["yg_%d_%d" % (t_, j_) for t_ in range(8) for j_ in range(RH)]
                    sqres = ["sq_%d_%d" % (t_, j_) for t_ in range(8) for j_ in range(RH)]
                    NC8 = 8 * RH

                    def st1(v):
                        return v.scalar_tensor_tensor(sscr[:, :NC8], ssum[:, :NC8], 1.0 / 128.0,
                                                      ssum[:, :NC8], ALU.mult, ALU.mult)
                    S.op("dve", st1, reads=ygres, writes=["sscr"])

                    def st2(v):
                        return v.tensor_tensor(sscr[:, :NC8], ssq[:, :NC8], sscr[:, :NC8],
                                               ALU.subtract)
                    S.op("dve", st2, reads=sqres + ["sscr"], writes=["sscr"])

                    def st3(v):
                        return v.tensor_scalar(sscr[:, :NC8], sscr[:, :NC8], 1.0 / 128.0, EPS,
                                               ALU.mult, ALU.add)
                    S.op("dve", st3, reads=["sscr"], writes=["sscr"])

                    def st4(v):
                        return v.reciprocal(srs[:, :NC8], sscr[:, :NC8])
                    S.op("dve", st4, reads=["sscr"], writes=["srs_p"])

                    def st5(v):
                        return v.tensor_scalar(smu[:, :NC8], ssum[:, :NC8], 1.0 / 128.0, None,
                                               ALU.mult)
                    S.op("dve", st5, reads=ygres, writes=["smu"])

                    def rsq(sc):
                        return sc.activation(srs[:, :NC8], srs[:, :NC8], ACT.Sqrt)
                    S.op("act", rsq, reads=["srs_p"], writes=["srs"])
                    if which == "q":
                        def dsc_(v):
                            return v.tensor_scalar(srs[:, :NC8], srs[:, :NC8], DSC, None, ALU.mult)
                        S.op("dve", dsc_, reads=["srs"], writes=["srs"])

                    def stcopy(v, grp=grp, mubuf=mubuf, rsbuf=rsbuf):
                        v.tensor_copy(rsbuf[:, grp * NC8:(grp + 1) * NC8], srs[:, :NC8])
                        return v.scalar_tensor_tensor(mubuf[:, grp * NC8:(grp + 1) * NC8],
                                                      smu[:, :NC8], -1.0, srs[:, :NC8],
                                                      ALU.mult, ALU.mult)
                    S.op("dve", stcopy, reads=["smu", "srs"], writes=[which + "must", "ssum_free"])
                    for tcg in range(8):
                        tc = grp * 8 + tcg
                        for j in range(RH):
                            col = grp * NC8 + tcg * RH + j

                            def appl(sc, tcg=tcg, j=j, col=col, mubuf=mubuf, rsbuf=rsbuf):
                                return sc.activation(ktmp[:, j * 128:(j + 1) * 128],
                                                     ygrp[:, tcg * W + j * 128: tcg * W + j * 128 + 128],
                                                     ACT.Identity, bias=mubuf[:, col:col + 1],
                                                     scale=rsbuf[:, col:col + 1])
                            S.op("act", appl, reads=["yg_%d_%d" % (tcg, j), which + "must"],
                                 writes=["ktmp%d" % j])
                            bank = 6 + (tcnt[0] % 2)
                            tcnt[0] += 1
                            pres = "pb%d" % bank

                            def tr(e, j=j, bank=bank):
                                return e.transpose(PBv[bank][:, 0:128], ktmp[:, j * 128:(j + 1) * 128],
                                                   ident16[:, :])
                            S.op("pe", tr, reads=["ktmp%d" % j, "ident16"], writes=[pres])

                            def cp(e, bank=bank, j=j, tc=tc, dst_list=dst_list):
                                return e.activation(dst_list[j][:, tc * 128:(tc + 1) * 128],
                                                    PBv[bank][:, 0:128], ACT.Copy)
                            S.op("act", cp, reads=[pres], writes=["dfm_%s%d" % ("kq"[dst_list is qfm], j)])

            for r in range(NH // RH):
                proj_fm(r, "k", wk, xt, NKC, kfm, kmu, krs)
                proj_fm(r, "q", wq, xqf, NQC, qfm, qmu, qrs)
                for j in range(RH):
                    hh = r * RH + j
                    src = vsp[hh, :, :, :]
                    dst = vaug[j][:, :].rearrange("p (c f) -> p c f", f=129)[:, :, 0:128]

                    def ldv(e, src=src, dst=dst):
                        return e.dma_start(out=dst, in_=src.rearrange("c p f -> p c f"))
                    S.dma("pool", ldv, reads=["vaug_free%d" % j], writes=["vaug%d" % j])
                for j in range(RH):
                    hh = r * RH + j
                    for ci in range(NQC):
                        nkci = 2 * ci + 2
                        ngrp = (nkci + 3) // 4

                        def emit_sgrp(g):
                            k0, k1 = g * 4, min(g * 4 + 4, nkci)
                            spres = "pb%d" % (g % 2)
                            spb = PB[g % 2]
                            for kc in range(k0, k1):
                                def smm(e, j=j, kc=kc, ci=ci, spb=spb):
                                    return e.matmul(spb[:, (kc % 4) * 128:(kc % 4) * 128 + 128],
                                                    kfm[j][:, kc * 128:(kc + 1) * 128],
                                                    qfm[j][:, ci * 128:(ci + 1) * 128],
                                                    start=True, stop=True)
                                S.op("pe", smm, reads=["dfm_k%d" % j, "dfm_q%d" % j], writes=[spres])
                        emit_sgrp(0)
                        for g in range(ngrp):
                            if g + 1 < ngrp:
                                emit_sgrp(g + 1)
                            k0 = g * 4
                            k1 = min(k0 + 4, nkci)
                            w = (k1 - k0) * 128
                            spres = "pb%d" % (g % 2)
                            spb = PB[g % 2]

                            def sqev(e, spb=spb, w=w, g=g):
                                return e.activation(asb[g % 2][:, :w], spb[:, :w], ACT.Square)
                            S.op("act", sqev, reads=[spres], writes=["asb%d" % (g % 2), spres + "f"])
                            for kc in range(k0, k1):
                                if kc == nkci - 2:
                                    def mka(v, kc=kc, g=g):
                                        return v.tensor_tensor(
                                            asb[g % 2][:, (kc % 4) * 128:(kc % 4) * 128 + 128],
                                            asb[g % 2][:, (kc % 4) * 128:(kc % 4) * 128 + 128],
                                            mA_sb[:, :], ALU.mult)
                                    S.op("dve", mka, reads=["asb%d" % (g % 2), "mA_sb"],
                                         writes=["asb%d" % (g % 2)])
                                elif kc == nkci - 1:
                                    def mkb(v, kc=kc, g=g):
                                        return v.tensor_tensor(
                                            asb[g % 2][:, (kc % 4) * 128:(kc % 4) * 128 + 128],
                                            asb[g % 2][:, (kc % 4) * 128:(kc % 4) * 128 + 128],
                                            mB_sb[:, :], ALU.mult)
                                    S.op("dve", mkb, reads=["asb%d" % (g % 2), "mB_sb"],
                                         writes=["asb%d" % (g % 2)])
                            for kc in range(k0, k1):
                                def avmm(e, j=j, kc=kc, g=g, nkci=nkci):
                                    return e.matmul(PB[2][:, 0:129],
                                                    asb[g % 2][:, (kc % 4) * 128:(kc % 4) * 128 + 128],
                                                    vaug[j][:, kc * 129: kc * 129 + 129],
                                                    start=(kc == 0), stop=(kc == nkci - 1))
                                S.op("pe", avmm, reads=["asb%d" % (g % 2), "vaug%d" % j], writes=["pb2"])

                        def zep(sc):
                            return sc.activation(zeps[:, :], PB[2][:, 128:129], ACT.Copy, bias=EPS)
                        S.op("act", zep, reads=["pb2"], writes=["zeps"])

                        def zev(v):
                            return v.reciprocal(zinv[:, :], zeps[:, :])
                        S.op("dve", zev, reads=["zeps"], writes=["zinv"])

                        def onor(sc):
                            return sc.activation(onorm[:, :], PB[2][:, 0:128], ACT.Copy,
                                                 scale=zinv[:, :])
                        S.op("act", onor, reads=["pb2", "zinv"], writes=["onorm", "pb2f"])
                        bank = 6 + (tcnt[0] % 2)
                        tcnt[0] += 1
                        pres = "pb%d" % bank

                        def otr(e, bank=bank):
                            return e.transpose(PBv[bank][:, 0:128], onorm[:, :], ident16[:, :])
                        S.op("pe", otr, reads=["onorm", "ident16"], writes=[pres])

                        def ocp(e, bank=bank, j=j, ci=ci):
                            return e.activation(ofm[j][:, ci * 128:(ci + 1) * 128],
                                                PBv[bank][:, 0:128], ACT.Copy)
                        S.op("act", ocp, reads=[pres], writes=["ofm%d" % j])
                        dq_step(1)

                    def spo(e, j=j, hh=hh):
                        return e.dma_start(out=osp[hh, :, :], in_=ofm[j][:, :])
                    S.dma("pool", spo, reads=["ofm%d" % j], writes=["osp%d" % hh, "vaug_free%d" % j],
                          sem="osps%d" % j)
        S.fence()
    dq_step(400)
    S.fence(include_nofence=True)
    dq_es.close()

    if "wo" in phases:
        with contextlib.ExitStack() as es:
            sb = lambda n, shape, dt: gsb(es, n, shape, dt)
            wo_sb = sb("wo_sb", [128, NDMC * HID], BF16)
            for oc in range(NDMC):
                def ldwo(e, oc=oc):
                    return e.dma_start(out=wo_sb[:, oc * HID:(oc + 1) * HID],
                                       in_=wo[oc * 128:(oc + 1) * 128, :])
                S.dma("pool", ldwo, reads=[], writes=["wo_sb"])
            oq = [sb("oq%d" % i, [128, 2048], BF16) for i in range(2)]
            xts3 = [sb("xts3_%d" % i, [128, 2048], F32) for i in range(2)]
            rstage = sb("rstage", [128, 2048], F32)
            r2scr = sb("r2scr", [128, 512], F32)
            lsum = sb("lsum", [128, 4], F32)
            lsq = sb("lsq", [128, 4], F32)
            lmu = sb("lmu", [128, 1], F32)
            lrs = sb("lrs", [128, 1], F32)
            lscr = sb("lscr", [128, 1], F32)
            x1t = sb("x1t", [128, 2048], F32)
            for qc in range(NQC):
                i = qc % 2
                bank0 = (qc % 2) * 4
                src = osp[:, :, qc * 128:(qc + 1) * 128]

                def ldo(e, src=src, i=i):
                    return e.dma_start(out=oq[i][:, :].rearrange("p (h f) -> p h f", f=128),
                                       in_=src.rearrange("h p f -> p h f"))
                S.dma("sp", ldo, reads=["oqfree%d" % i], writes=["oq%d" % i])

                def ldxq(e, qc=qc, i=i):
                    return e.dma_start(out=xts3[i][:, :], in_=xqt[qc * 128:(qc + 1) * 128, :])
                S.dma("sp", ldxq, reads=["xts3free%d" % i], writes=["xts3_%d" % i])
                for dmt in range(4):
                    pres = "pb%d" % (bank0 + dmt)
                    pb = PB[bank0 + dmt]
                    for hh in range(NH):
                        def mm(e, i=i, hh=hh, pb=pb, dmt=dmt):
                            return e.matmul(pb[:, :], oq[i][:, hh * 128:(hh + 1) * 128],
                                            wo_sb[:, hh * HID + dmt * 512: hh * HID + dmt * 512 + 512],
                                            start=(hh == 0), stop=(hh == NH - 1))
                        S.op("pe", mm, reads=["oq%d" % i, "wo_sb"],
                             writes=[pres] + (["oqfree%d" % i] if (hh == NH - 1 and dmt == 3) else []))

                    def radd(e, pb=pb, dmt=dmt, i=i):
                        return e.scalar_tensor_tensor(rstage[:, dmt * 512:(dmt + 1) * 512], pb[:, :], 1.0,
                                                      xts3[i][:, dmt * 512:(dmt + 1) * 512], ALU.mult, ALU.add,
                                                      accum_out=lsum[:, dmt:dmt + 1])
                    S.op("dve", radd, reads=[pres, "xts3_%d" % i],
                         writes=["rstage%d" % dmt] + (["xts3free%d" % i] if dmt == 3 else []))

                    def rsq(e, dmt=dmt):
                        return e.activation(r2scr[:, :], rstage[:, dmt * 512:(dmt + 1) * 512], ACT.Square,
                                            accum_out=lsq[:, dmt:dmt + 1])
                    S.op("act", rsq, reads=["rstage%d" % dmt], writes=["lsq%d" % dmt])

                def ln1(v):
                    return v.tensor_reduce(lmu[:, :], lsum[:, :], AX.X, ALU.add)
                S.op("dve", ln1, reads=["rstage%d" % d_ for d_ in range(4)], writes=["lmu"])

                def ln2(v):
                    return v.tensor_reduce(lscr[:, :], lsq[:, :], AX.X, ALU.add)
                S.op("dve", ln2, reads=["lsq%d" % d_ for d_ in range(4)], writes=["lscr"])

                def ln3(v):
                    return v.scalar_tensor_tensor(lrs[:, :], lmu[:, :], 1.0 / 2048.0,
                                                  lmu[:, :], ALU.mult, ALU.mult)
                S.op("dve", ln3, reads=["lmu"], writes=["lrs_t"])

                def ln4(v):
                    return v.tensor_tensor(lscr[:, :], lscr[:, :], lrs[:, :], ALU.subtract)
                S.op("dve", ln4, reads=["lscr", "lrs_t"], writes=["lscr"])

                def ln5(v):
                    return v.tensor_scalar(lscr[:, :], lscr[:, :], 1.0 / 2048.0, EPS,
                                           ALU.mult, ALU.add)
                S.op("dve", ln5, reads=["lscr"], writes=["lscr"])

                def ln6(v):
                    return v.reciprocal(lrs[:, :], lscr[:, :])
                S.op("dve", ln6, reads=["lscr"], writes=["lrs_p"])

                def ln7(v):
                    return v.tensor_scalar(lmu[:, :], lmu[:, :], 1.0 / 2048.0, None, ALU.mult)
                S.op("dve", ln7, reads=["lmu"], writes=["lmu"])

                def lnsq(sc):
                    return sc.activation(lrs[:, :], lrs[:, :], ACT.Sqrt)
                S.op("act", lnsq, reads=["lrs_p"], writes=["lrs"])

                def applb(v):
                    return v.scalar_tensor_tensor(lscr[:, :], lmu[:, :], -1.0, lrs[:, :],
                                                  ALU.mult, ALU.mult)
                S.op("dve", applb, reads=["lmu", "lrs"], writes=["lnb"])

                def appl(sc):
                    return sc.activation(x1t[:, :], rstage[:, :], ACT.Identity,
                                         bias=lscr[:, :], scale=lrs[:, :])
                S.op("act", appl, reads=["rstage%d" % d_ for d_ in range(4)] + ["lnb", "lrs"],
                     writes=["x1t"])

                def spx1(e, qc=qc):
                    return e.dma_start(out=x1s[qc, :, :], in_=x1t[:, :])
                S.dma("pool", spx1, reads=["x1t"], writes=["x1sp%d" % qc], sem="x1sp")
        S.fence()

    if "ffn" in phases:
        with contextlib.ExitStack() as es:
            sb = lambda n, shape, dt: gsb(es, n, shape, dt)
            qfs = [alloc_quant_bufs(es, nc, 512, "qf%d" % i) for i in range(2)]
            x1r = sb("x1r", [128, 2048], F32)
            x1res = x1r
            xdq_t = sb("xdq_t", [128, 2048], BF16)
            xdq = sb("xdq", [128, NDMC * 512], BF16)
            gslab = [sb("gslab%d" % i, [128, NDMC * 256], BF16) for i in range(2)]
            uslab = [sb("uslab%d" % i, [128, NDMC * 256], BF16) for i in range(2)]
            dslab = [sb("dslab%d" % i, [128, HID], BF16) for i in range(2)]
            sgbuf = [sb("sgbuf%d" % i, [128, 256], F32) for i in range(2)]
            hbuf = [sb("hbuf%d" % i, [128, 256], F32) for i in range(2)]
            hdq_t = [sb("hdq_t%d" % i, [128, 256], BF16) for i in range(2)]
            hdq = sb("hdq", [128, NIC * 512], BF16)
            f2scr = sb("f2scr", [128, 512], F32)
            fsum = sb("fsum", [128, 4], F32)
            fsq = sb("fsq", [128, 4], F32)
            fmu = sb("fmu", [128, 1], F32)
            frs = sb("frs", [128, 1], F32)
            fscr = sb("fscr", [128, 1], F32)
            outt = sb("outt", [128, 2048], F32)

            for tile in range(4):
                for qc4 in range(4):
                    qc = tile * 4 + qc4

                    def ldx1(e, qc=qc):
                        return e.dma_start(out=x1r[:, :], in_=x1s[qc, :, :])
                    S.dma("sp", ldx1, reads=["x1rfree"], writes=["x1r"])
                    for h in range(4):
                        emit_quant(S, qfs[h % 2], "qf%d" % (h % 2), x1r[:, h * 512:(h + 1) * 512], "x1r",
                                   xdq_t[:, h * 512:(h + 1) * 512], "xdq_t", 512)
                    for dmc in range(NDMC):
                        bank = 6 + (tcnt[0] % 2)
                        tcnt[0] += 1
                        pres = "pb%d" % bank

                        def tr(e, dmc=dmc, bank=bank):
                            return e.transpose(PBv[bank][:, 0:128], xdq_t[:, dmc * 128:(dmc + 1) * 128],
                                               ident16[:, :])
                        S.op("pe", tr, reads=["xdq_t", "ident16"],
                             writes=[pres] + (["x1rfree"] if dmc == NDMC - 1 else []))

                        def cp(e, bank=bank, dmc=dmc, qc4=qc4):
                            return e.activation(xdq[:, dmc * 512 + qc4 * 128: dmc * 512 + qc4 * 128 + 128],
                                                PBv[bank][:, 0:128], ACT.Copy)
                        S.op("act", cp, reads=[pres], writes=["xdq"])
                for ig in range(32):
                    sl = ig % 2
                    for nm, slab, wspl in (("g", gslab, gws), ("u", uslab, uws)):
                        src = wspl[:, :, ig * 256:(ig + 1) * 256]
                        for piece in range(4):
                            def ldsl(e, src=src, slab=slab, piece=piece, sl=sl):
                                return e.dma_start(
                                    out=slab[sl][:, piece * 4 * 256:(piece + 1) * 4 * 256].rearrange(
                                        "p (c f) -> p c f", f=256),
                                    in_=src[piece * 4:(piece + 1) * 4].rearrange("c p f -> p c f"))
                            S.dma("sp", ldsl, reads=[nm + "slabfree%d" % sl], writes=[nm + "slab%d" % sl])
                    for ph in range(2):
                        for q2 in range(2):
                            qc4 = 2 * ph + q2
                            presg = "pb%d" % (q2 * 2)
                            presu = "pb%d" % (q2 * 2 + 1)
                            pbg = PB[q2 * 2]
                            pbu = PB[q2 * 2 + 1]
                            for dmc in range(NDMC):
                                def mmg(e, dmc=dmc, qc4=qc4, pbg=pbg, pbu=pbu, sl=sl):
                                    e.matmul(pbg[:, 0:256],
                                             xdq[:, dmc * 512 + qc4 * 128: dmc * 512 + qc4 * 128 + 128],
                                             gslab[sl][:, dmc * 256:(dmc + 1) * 256],
                                             start=(dmc == 0), stop=(dmc == NDMC - 1))
                                    return e.matmul(pbu[:, 0:256],
                                                    xdq[:, dmc * 512 + qc4 * 128: dmc * 512 + qc4 * 128 + 128],
                                                    uslab[sl][:, dmc * 256:(dmc + 1) * 256],
                                                    start=(dmc == 0), stop=(dmc == NDMC - 1))
                                S.op("pe", mmg,
                                     reads=["xdq", "gslab%d" % sl, "uslab%d" % sl],
                                     writes=[presg, presu] + (["gslabfree%d" % sl, "uslabfree%d" % sl]
                                                              if (dmc == NDMC - 1 and ph == 1 and q2 == 1)
                                                              else []))
                        for q2 in range(2):
                            qc4 = 2 * ph + q2
                            presg = "pb%d" % (q2 * 2)
                            presu = "pb%d" % (q2 * 2 + 1)
                            pbg = PB[q2 * 2]
                            pbu = PB[q2 * 2 + 1]

                            def silu(e, pbg=pbg, q2=q2):
                                return e.activation(sgbuf[q2][:, :], pbg[:, 0:256], ACT.Silu)
                            S.op("act", silu, reads=[presg], writes=["sgbuf%d" % q2, presg + "f"])

                            def hmul(v, pbu=pbu, q2=q2):
                                return v.tensor_tensor(hbuf[q2][:, :], sgbuf[q2][:, :],
                                                       pbu[:, 0:256], ALU.mult)
                            S.op("dve", hmul, reads=["sgbuf%d" % q2, presu],
                                 writes=["hbuf%d" % q2, presu + "f"])
                        emit_quant_multi(S, [
                            (qfs[q2], "qf%d" % q2, hbuf[q2][:, :256], "hbuf%d" % q2,
                             hdq_t[q2][:, :256], "hdq_t%d" % q2, 256)
                            for q2 in range(2)])
                        for q2 in range(2):
                            qc4 = 2 * ph + q2
                            for t in range(2):
                                ic = ig * 2 + t
                                bank = 6 + (tcnt[0] % 2)
                                tcnt[0] += 1
                                pres = "pb%d" % bank

                                def tr(e, t=t, bank=bank, q2=q2):
                                    return e.transpose(PBv[bank][:, 0:128],
                                                       hdq_t[q2][:, t * 128:(t + 1) * 128],
                                                       ident16[:, :])
                                S.op("pe", tr, reads=["hdq_t%d" % q2, "ident16"], writes=[pres])

                                def cp(e, bank=bank, ic=ic, qc4=qc4):
                                    return e.activation(
                                        hdq[:, ic * 512 + qc4 * 128: ic * 512 + qc4 * 128 + 128],
                                        PBv[bank][:, 0:128], ACT.Copy)
                                S.op("act", cp, reads=[pres], writes=["hdq"])
                for half in range(2):
                    for ic in range(NIC):
                        i = ic % 2

                        def lddw(e, ic=ic, i=i):
                            return e.dma_start(out=dslab[i][:, :], in_=dws[ic, :, :])
                        S.dma("sp", lddw, reads=["dslabfree%d" % i], writes=["dslab%d" % i])
                        for q2 in range(2):
                            qc4 = half * 2 + q2

                            def mmd(e, ic=ic, i=i, qc4=qc4, q2=q2):
                                last = None
                                for dmt in range(4):
                                    last = e.matmul(
                                        PB[q2 * 4 + dmt][:, :],
                                        hdq[:, ic * 512 + qc4 * 128: ic * 512 + qc4 * 128 + 128],
                                        dslab[i][:, dmt * 512:(dmt + 1) * 512],
                                        start=(ic == 0), stop=(ic == NIC - 1))
                                return last
                            S.op("pe", mmd, reads=["hdq", "dslab%d" % i],
                                 writes=["pbd%d" % q2] + (["dslabfree%d" % i] if q2 == 1 else []))
                    for q2 in range(2):
                        qc4 = half * 2 + q2
                        qc = tile * 4 + qc4

                        def ldx1b(e, qc=qc):
                            return e.dma_start(out=x1res[:, :], in_=x1s[qc, :, :])
                        S.dma("sp", ldx1b, reads=["x1rfree"], writes=["x1r"])
                        for dmt in range(4):
                            pb = PB[q2 * 4 + dmt]

                            def radd(e, pb=pb, dmt=dmt):
                                return e.scalar_tensor_tensor(
                                    outt[:, dmt * 512:(dmt + 1) * 512], pb[:, :], 1.0,
                                    x1res[:, dmt * 512:(dmt + 1) * 512], ALU.mult, ALU.add,
                                    accum_out=fsum[:, dmt:dmt + 1])
                            S.op("dve", radd, reads=["pbd%d" % q2, "x1r"],
                                 writes=["outt%d" % dmt] + (["x1rfree"] if dmt == 3 else []))

                            def rsq(e, dmt=dmt):
                                return e.activation(f2scr[:, :], outt[:, dmt * 512:(dmt + 1) * 512],
                                                    ACT.Square, accum_out=fsq[:, dmt:dmt + 1])
                            S.op("act", rsq, reads=["outt%d" % dmt], writes=["fsq%d" % dmt])

                        def fn1(v):
                            return v.tensor_reduce(fmu[:, :], fsum[:, :], AX.X, ALU.add)
                        S.op("dve", fn1, reads=["outt%d" % d_ for d_ in range(4)], writes=["fmu"])

                        def fn2(v):
                            return v.tensor_reduce(fscr[:, :], fsq[:, :], AX.X, ALU.add)
                        S.op("dve", fn2, reads=["fsq%d" % d_ for d_ in range(4)], writes=["fscr"])

                        def fn3(v):
                            return v.scalar_tensor_tensor(frs[:, :], fmu[:, :], 1.0 / 2048.0,
                                                          fmu[:, :], ALU.mult, ALU.mult)
                        S.op("dve", fn3, reads=["fmu"], writes=["frs_t"])

                        def fn4(v):
                            return v.tensor_tensor(fscr[:, :], fscr[:, :], frs[:, :], ALU.subtract)
                        S.op("dve", fn4, reads=["fscr", "frs_t"], writes=["fscr"])

                        def fn5(v):
                            return v.tensor_scalar(fscr[:, :], fscr[:, :], 1.0 / 2048.0, EPS,
                                                   ALU.mult, ALU.add)
                        S.op("dve", fn5, reads=["fscr"], writes=["fscr"])

                        def fn6(v):
                            return v.reciprocal(frs[:, :], fscr[:, :])
                        S.op("dve", fn6, reads=["fscr"], writes=["frs_p"])

                        def fn7(v):
                            return v.tensor_scalar(fmu[:, :], fmu[:, :], 1.0 / 2048.0, None,
                                                   ALU.mult)
                        S.op("dve", fn7, reads=["fmu"], writes=["fmu"])

                        def lnsq(sc):
                            return sc.activation(frs[:, :], frs[:, :], ACT.Sqrt)
                        S.op("act", lnsq, reads=["frs_p"], writes=["frs"])

                        def applb(v):
                            return v.scalar_tensor_tensor(fscr[:, :], fmu[:, :], -1.0, frs[:, :],
                                                          ALU.mult, ALU.mult)
                        S.op("dve", applb, reads=["fmu", "frs"], writes=["fnb"])

                        def appl(sc):
                            return sc.activation(outt[:, :], outt[:, :], ACT.Identity,
                                                 bias=fscr[:, :], scale=frs[:, :])
                        S.op("act", appl, reads=["outt%d" % d_ for d_ in range(4)] + ["fnb", "frs"],
                             writes=["outt0", "outt1", "outt2", "outt3"])

                        def spout(e, qc=qc):
                            return e.dma_start(out=out[qc * 128:(qc + 1) * 128, :], in_=outt[:, :])
                        S.dma("pool", spout, reads=["outt%d" % d_ for d_ in range(4)],
                              writes=["outd%d" % qc], sem="outsp")
        S.fence()

    o = S.op("sp", None, reads=[], writes=[])
    o.waits = S.final_waits()
    o.deps = set()

    S.emit(nc, root)
    root.close()
    return nc


def core_inputs(core, x, Wq, Wk, Wv, Wo, gamma, beta, gate_w, up_w, down_w):
    b = core // 2
    half = core % 2
    xb = np.ascontiguousarray(x[b])
    xtb = np.ascontiguousarray(xb.T)
    qcols = np.concatenate([np.arange(c * 128, (c + 1) * 128)
                            for c in range(half, 32, 2)])
    tri = np.triu(np.ones((128, 128), dtype=np.float32))
    if half == 0:
        mA, mB = tri, np.zeros((128, 128), np.float32)
    else:
        mA, mB = np.ones((128, 128), np.float32), tri
    d = {
        "xt": xtb,
        "xqf": np.ascontiguousarray(xtb[:, qcols]),
        "xqt": np.ascontiguousarray(xb[qcols]),
        "wqT": np.ascontiguousarray(Wq.T),
        "wkT": np.ascontiguousarray(Wk.T),
        "wvT": np.ascontiguousarray(Wv.T),
        "woT": np.ascontiguousarray(Wo.T),
        "gw": np.ascontiguousarray(gate_w),
        "uw": np.ascontiguousarray(up_w),
        "dw": np.ascontiguousarray(down_w),
        "gb2": np.ascontiguousarray(np.broadcast_to(np.tile(gamma, 4)[None, :], (128, 512))),
        "bb2": np.ascontiguousarray(np.broadcast_to(np.tile(beta, 4)[None, :], (128, 512))),
        "mA": mA,
        "mB": mB,
        "idin": np.eye(128, dtype=np.float32),
    }
    return d, qcols


_NC_CACHE = {}


def _get_nc():
    if "nc" not in _NC_CACHE:
        _NC_CACHE["nc"] = build_program({})
    return _NC_CACHE["nc"]


def kernel(L0_attn_hidden_states, L0_attn_Wq, L0_attn_Wk, L0_attn_Wv, L0_attn_Wo,
           L0_attn_gamma, L0_attn_beta, L0_ffn_gate_w, L0_ffn_up_w, L0_ffn_down_w):
    from concourse.bass_utils import run_bass_kernel_spmd
    x = np.asarray(L0_attn_hidden_states, dtype=np.float32)
    args = (x, np.asarray(L0_attn_Wq, np.float32), np.asarray(L0_attn_Wk, np.float32),
            np.asarray(L0_attn_Wv, np.float32), np.asarray(L0_attn_Wo, np.float32),
            np.asarray(L0_attn_gamma, np.float32), np.asarray(L0_attn_beta, np.float32),
            np.asarray(L0_ffn_gate_w, np.float32), np.asarray(L0_ffn_up_w, np.float32),
            np.asarray(L0_ffn_down_w, np.float32))
    nc = _get_nc()
    ims = []
    qcls = []
    for c in range(8):
        d, qc = core_inputs(c, *args)
        ims.append(d)
        qcls.append(qc)
    res = run_bass_kernel_spmd(nc, ims, core_ids=list(range(8)))
    out = np.empty((4, 4096, 2048), dtype=np.float32)
    for c in range(8):
        out[c // 2][qcls[c]] = np.asarray(res.results[c]["outp"], dtype=np.float32)
    return out


# version 06
# speedup vs baseline: 1.0973x; 1.0251x over previous
import contextlib
import numpy as np
import concourse.bass as bass
import concourse.mybir as mybir

F32 = mybir.dt.float32
I32 = mybir.dt.int32
BF16 = mybir.dt.bfloat16
ALU = mybir.AluOpType
ACT = mybir.ActivationFunctionType
AX = mybir.AxisListType

HID = 2048
TKV = 4096
NH = 16
DH = 128
II = 8192
NQC = 16
NKC = 32
NDMC = 16
NIC = 64
EPS = 1e-5
DSC = float(DH) ** -0.5

ENGINES = ("pe", "act", "dve", "pool", "sp")


class Op:
    __slots__ = ("eng", "fn", "reads", "writes", "dma", "sem_grp", "bulk", "idx",
                 "waits", "need_inc", "deps", "inc_val", "same_max", "nofence")

    def __init__(self, eng, fn, reads, writes, dma, sem_grp=None, bulk=False):
        self.eng = eng
        self.fn = fn
        self.reads = list(reads)
        self.writes = list(writes)
        self.dma = dma
        self.sem_grp = sem_grp
        self.bulk = bulk
        self.idx = None
        self.waits = []
        self.need_inc = False
        self.deps = set()
        self.inc_val = 0
        self.same_max = 0
        self.nofence = False


class Sched:

    def __init__(self):
        self.ops = []
        self.last_w = {}
        self.readers = {}
        self.ngrp = {}
        self.neng = {e: 0 for e in ENGINES}
        self.fence_snapshot = []
        self.last_op = {e: None for e in ENGINES}
        self.nofence_groups = set()

    def op(self, eng, fn, reads=(), writes=(), dma=None, sem_grp=None, bulk=False,
           nofence=False):
        o = Op(eng, fn, reads, writes, dma, sem_grp, bulk)
        o.nofence = nofence
        if nofence and dma is not None:
            self.nofence_groups.add(sem_grp)
        if dma is not None:
            self.ngrp[sem_grp] = self.ngrp.get(sem_grp, 0) + 1
            o.idx = self.ngrp[sem_grp]
        else:
            self.neng[eng] += 1
            o.idx = self.neng[eng]
        deps = set()
        for r in o.reads:
            w = self.last_w.get(r)
            if w is not None:
                deps.add(w)
        for r in o.writes:
            w = self.last_w.get(r)
            if w is not None:
                deps.add(w)
            for rd in self.readers.get(r, ()):
                deps.add(rd)
        if not nofence:
            for fop in self.fence_snapshot:
                deps.add(fop)
        o.deps = deps
        self.ops.append(o)
        for r in o.writes:
            self.last_w[r] = o
            self.readers[r] = []
        for r in o.reads:
            self.readers.setdefault(r, []).append(o)
        if dma is None:
            self.last_op[eng] = o
        return o

    def dma(self, eng, fn, reads=(), writes=(), sem=None, bulk=False, nofence=False):
        grp = sem if sem is not None else (writes[0] if writes else "misc")
        return self.op(eng, fn, reads, writes, dma=eng, sem_grp=grp, bulk=bulk,
                       nofence=nofence)

    def fence(self, include_nofence=False):
        snap = []
        for e in ENGINES:
            if self.last_op[e] is not None:
                snap.append(self.last_op[e])
        seen = set()
        for o in reversed(self.ops):
            if o.dma is not None and o.sem_grp not in seen:
                if o.sem_grp in self.nofence_groups and not include_nofence:
                    continue
                seen.add(o.sem_grp)
                snap.append(o)
        self.fence_snapshot = snap
        for o in snap:
            if o.dma is not None:
                o.bulk = True

    def resolve(self):
        totals = dict(self.ngrp)
        for o in self.ops:
            for d in o.deps:
                if d is o:
                    continue
                if d.dma is not None:
                    v = 16 * (totals[d.sem_grp] if d.bulk else d.idx)
                    o.waits.append((("g", d.sem_grp), v))
                elif d.eng != o.eng:
                    d.need_inc = True
                    o.waits.append((("eng", d.eng), d))
                else:
                    if o.eng in ("dve", "act"):
                        d.need_inc = True
                        o.waits.append((("eng", d.eng), d))
        cnt = {e: 0 for e in ENGINES}
        for o in self.ops:
            if o.dma is None and o.need_inc:
                cnt[o.eng] += 1
            o.inc_val = cnt[o.eng]
        for o in self.ops:
            o.waits = [(k, (v if isinstance(v, int) else v.inc_val)) for k, v in o.waits]

    def final_waits(self):
        return [(("g", g), 16 * n) for g, n in self.ngrp.items()]

    def emit(self, nc, es):
        self.resolve()
        sems = {("eng", e): es.enter_context(nc.semaphore("sem_" + e)) for e in ENGINES}
        for g in self.ngrp:
            sems[("g", g)] = es.enter_context(nc.semaphore("sg_" + g))
        per_eng = {e: [] for e in ENGINES}
        for o in self.ops:
            per_eng[o.eng].append(o)
        block = es.enter_context(nc.Block())

        def mk(eng_name):
            ops = per_eng[eng_name]

            def prog(eng):
                hi = {}
                for o in ops:
                    for k, v in o.waits:
                        if v > hi.get(k, 0):
                            hi[k] = v
                            eng.wait_ge(sems[k], v)
                    if o.fn is None:
                        continue
                    inst = o.fn(eng)
                    if o.dma is not None:
                        inst.then_inc(sems[("g", o.sem_grp)], 16)
                    elif o.need_inc:
                        inst.then_inc(sems[("eng", o.eng)], 1)
            return prog

        block.tensor(mk("pe"))
        block.scalar(mk("act"))
        block.vector(mk("dve"))
        block.gpsimd(mk("pool"))
        block.sync(mk("sp"))


def quant_stages(b, tag, src_ap, src_res, dq_ap, dq_res, C):
    nb = C // 32
    B = lambda n: b[n][:, :C]
    Bb = lambda n: b[n][:, :nb]
    rs = lambda *names: [tag + n for n in names]

    def st_red(v):
        s3 = src_ap.rearrange("p (b w) -> p b w", w=32)
        return v.tensor_reduce(Bb("m"), s3, AX.X, ALU.max, apply_absolute_value=True)

    def st_s(v):
        return v.tensor_scalar(Bb("s"), Bb("m"), 1.0 / 448.0, 1e-30, ALU.mult, ALU.max)

    def st_si(v):
        return v.reciprocal(Bb("sinv"), Bb("s"))

    def st_y(v):
        s3 = src_ap.rearrange("p (b w) -> p b w", w=32)
        sb = Bb("sinv").unsqueeze(2).broadcast_to([128, nb, 32])
        return v.tensor_tensor(B("y").rearrange("p (b w) -> p b w", w=32), s3, sb, ALU.mult)

    def st_ei(v):
        return v.tensor_scalar(B("ei"), B("y").bitcast(I32), 23, 255,
                               ALU.logical_shift_right, ALU.bitwise_and)

    def st_ub(v):
        return v.tensor_scalar(B("ub"), B("ei"), 121.0, 8388608.0, ALU.max, ALU.mult)

    def st_ui(sc):
        return sc.activation(B("ui"), B("ub"), ACT.Copy, bias=2155872256.0, scale=-1.0)

    def st_ri(v):
        return v.tensor_tensor(B("ri"), B("y"), B("ui").bitcast(F32), ALU.mult)

    def st_rf(sc):
        return sc.activation(B("rf"), B("ri"), ACT.Copy, bias=0.0, scale=0.125)

    def st_q(v):
        return v.tensor_tensor(B("q"), B("rf"), B("ub").bitcast(F32), ALU.mult)

    def st_dq(v):
        sb2 = Bb("s").unsqueeze(2).broadcast_to([128, nb, 32])
        return v.tensor_tensor(dq_ap.rearrange("p (b w) -> p b w", w=32),
                               B("q").rearrange("p (b w) -> p b w", w=32), sb2, ALU.mult)

    return [
        ("dve", st_red, [src_res], rs("m")),
        ("dve", st_s, rs("m"), rs("s")),
        ("dve", st_si, rs("s"), rs("sinv")),
        ("dve", st_y, [src_res] + rs("sinv"), rs("y")),
        ("dve", st_ei, rs("y"), rs("ei")),
        ("dve", st_ub, rs("ei"), rs("ub")),
        ("act", st_ui, rs("ub"), rs("ui")),
        ("dve", st_ri, rs("y", "ui"), rs("ri")),
        ("act", st_rf, rs("ri"), rs("rf")),
        ("dve", st_q, rs("rf", "ub"), rs("q")),
        ("dve", st_dq, rs("q", "s"), [dq_res]),
    ]


def emit_quant_multi(S, jobs, nofence=False):
    stages = [quant_stages(*j) for j in jobs]
    for si in range(11):
        for st in stages:
            eng, fn, rd, wr = st[si]
            S.op(eng, fn, reads=rd, writes=wr, nofence=nofence)


def emit_quant(S, b, tag, src_ap, src_res, dq_ap, dq_res, C):
    emit_quant_multi(S, [(b, tag, src_ap, src_res, dq_ap, dq_res, C)])


def alloc_quant_bufs(es, nc, C, tag):
    sb = lambda n, dt, w=C: es.enter_context(nc.sbuf_tensor(tag + n, [128, w], dt))
    b = {}
    b["m"] = sb("m", F32, C // 32)
    b["s"] = sb("s", F32, C // 32)
    b["sinv"] = sb("si", F32, C // 32)
    b["y"] = sb("y", F32)
    b["ei"] = sb("ei", I32)
    b["ub"] = sb("ub", I32)
    b["ui"] = sb("ui", I32)
    b["ri"] = sb("ri", I32)
    b["rf"] = sb("rf", F32)
    b["q"] = sb("q", F32)
    return b


def build_program(cfg=None):
    cfg = cfg or {}
    nc = bass.Bass("TRN2", target_bir_lowering=False)
    root = contextlib.ExitStack()
    S = Sched()
    phases = cfg.get("phases", ("dequant", "v", "attn", "wo", "ffn"))
    tap = cfg.get("tap", None)

    xt = nc.dram_tensor("xt", [HID, TKV], F32, kind="ExternalInput")
    xqf = nc.dram_tensor("xqf", [HID, 2048], F32, kind="ExternalInput")
    xqt = nc.dram_tensor("xqt", [2048, HID], F32, kind="ExternalInput")
    wq = nc.dram_tensor("wqT", [HID, HID], F32, kind="ExternalInput")
    wk = nc.dram_tensor("wkT", [HID, HID], F32, kind="ExternalInput")
    wv = nc.dram_tensor("wvT", [HID, HID], F32, kind="ExternalInput")
    wo = nc.dram_tensor("woT", [HID, HID], F32, kind="ExternalInput")
    gwd = nc.dram_tensor("gw", [II, HID], F32, kind="ExternalInput")
    uwd = nc.dram_tensor("uw", [II, HID], F32, kind="ExternalInput")
    dwd = nc.dram_tensor("dw", [HID, II], F32, kind="ExternalInput")
    gb2 = nc.dram_tensor("gb2", [128, 512], F32, kind="ExternalInput")
    bb2 = nc.dram_tensor("bb2", [128, 512], F32, kind="ExternalInput")
    mAd = nc.dram_tensor("mA", [128, 128], F32, kind="ExternalInput")
    mBd = nc.dram_tensor("mB", [128, 128], F32, kind="ExternalInput")
    idin = nc.dram_tensor("idin", [128, 128], F32, kind="ExternalInput")
    out = nc.dram_tensor("outp", [2048, HID], F32, kind="ExternalOutput")

    ik = lambda t: "ExternalOutput" if tap == t else "Internal"
    gws = nc.dram_tensor("gws", [NDMC, 128, II], BF16, kind=ik("dq"))
    uws = nc.dram_tensor("uws", [NDMC, 128, II], BF16, kind=ik("dq"))
    dws = nc.dram_tensor("dws", [NIC, 128, HID], BF16, kind=ik("dq"))
    vsp = nc.dram_tensor("vsp", [NH, NKC, 128, 128], BF16, kind=ik("v"))
    osp = nc.dram_tensor("osp", [NH, 128, 2048], BF16,
                         kind=("ExternalOutput" if tap in ("o", "azo") else "Internal"))
    x1s = nc.dram_tensor("x1s", [NQC, 128, HID], F32, kind=ik("x1"))
    kqd = nc.dram_tensor("kqd", [8, 128, TKV], BF16, kind=ik("kq"))
    azd = nc.dram_tensor("azd", [4, 128, 512], F32,
                         kind=("ExternalOutput" if tap in ("az", "azo") else "Internal"))

    gsb = lambda es_, n, shape, dt: es_.enter_context(nc.sbuf_tensor(n, shape, dt))
    PB = [root.enter_context(nc.psum_tensor("pb%d" % i, [128, 512], F32)) for i in range(8)]
    PBv = [PB[i][:, :].bitcast(BF16) for i in range(8)]

    ident16 = gsb(root, "ident16", [128, 128], BF16)
    S.dma("pool", lambda e: e.dma_start(out=ident16[:, :], in_=idin[:, :]),
          reads=[], writes=["ident16"])

    tcnt = [0]

    def transpose_to(src_ap_list, src_res, dst_ap_of, dst_res):
        n = len(src_ap_list)
        for base in range(0, n, 4):
            m = min(4, n - base)
            bank = 6 + (tcnt[0] % 2)
            tcnt[0] += 1
            pres = "pb%d" % bank
            for t in range(m):
                def tr(e, t=t, base=base, bank=bank):
                    return e.transpose(PBv[bank][:, (t * 128):(t * 128 + 128)],
                                       src_ap_list[base + t], ident16[:, :])
                S.op("pe", tr, reads=[src_res, "ident16"],
                     writes=[pres] if t == m - 1 else [pres])

            def cp(e, base=base, m=m, bank=bank):
                return e.activation(dst_ap_of(base, m), PBv[bank][:, 0:m * 128], ACT.Copy)
            S.op("act", cp, reads=[pres], writes=[dst_res])

    dq_gen_obj = None
    dq_es = contextlib.ExitStack()
    if "dequant" in phases:
        sbd = lambda n, shape, dt: gsb(dq_es, n, shape, dt)
        qbs = [alloc_quant_bufs(dq_es, nc, 1024, "qd%d" % i) for i in range(2)]
        raw = [sbd("rawd%d" % i, [128, 2048], F32) for i in range(2)]
        dqc = [sbd("dqcb%d" % i, [128, 1024], BF16) for i in range(2)]
        band = sbd("band", [128, 16 * 512], BF16)

        def dq_gen():
            for wname, win, wsp, nrc, nslab in (("gw", gwd, gws, 64, 1), ("uw", uwd, uws, 64, 1),
                                                ("dw", dwd, dws, 16, 4)):
                for slab in range(nslab):
                    for bandi in range(nrc // 4):
                        for rc4 in range(4):
                            rc = bandi * 4 + rc4
                            i = rc % 2
                            src = win[rc * 128:(rc + 1) * 128, slab * 2048:(slab + 1) * 2048]

                            def ld(e, src=src, i=i):
                                return e.dma_start(out=raw[i][:, :], in_=src)
                            S.dma("sp", ld, reads=[], writes=["rawd%d" % i], nofence=True)
                            emit_quant_multi(S, [
                                (qbs[h], "qd%d" % h, raw[i][:, h * 1024:(h + 1) * 1024],
                                 "rawd%d" % i, dqc[h][:, :], "dqcq%d" % h, 1024)
                                for h in range(2)], nofence=True)
                            for h in range(2):
                                for t in range(8):
                                    dchunk = h * 8 + t
                                    bank = 6 + (tcnt[0] % 2)
                                    tcnt[0] += 1
                                    pres = "pb%d" % bank

                                    def tr(e, h=h, t=t, bank=bank):
                                        return e.transpose(PBv[bank][:, 0:128],
                                                           dqc[h][:, t * 128:(t + 1) * 128],
                                                           ident16[:, :])
                                    S.op("pe", tr, reads=["dqcq%d" % h, "ident16"], writes=[pres],
                                         nofence=True)

                                    def cp(e, bank=bank, dchunk=dchunk, rc4=rc4):
                                        return e.activation(
                                            band[:, dchunk * 512 + rc4 * 128: dchunk * 512 + rc4 * 128 + 128],
                                            PBv[bank][:, 0:128], ACT.Copy)
                                    S.op("act", cp, reads=[pres], writes=["band_c%d" % dchunk],
                                         nofence=True)
                            yield
                        for dchunk in range(16):
                            dst = wsp[slab * 16 + dchunk, :, bandi * 512:(bandi + 1) * 512]

                            def sp_(e, dst=dst, dchunk=dchunk):
                                return e.dma_start(out=dst, in_=band[:, dchunk * 512: (dchunk + 1) * 512])
                            S.dma("pool", sp_, reads=["band_c%d" % dchunk],
                                  writes=["%s_sp%d_%d" % (wname, slab * 16 + dchunk, bandi)],
                                  sem="wsps%d" % dchunk, nofence=True)
                        yield
        dq_gen_obj = dq_gen()

    def dq_step(n=1):
        if dq_gen_obj is not None:
            for _ in range(n):
                if next(dq_gen_obj, "END") == "END":
                    break

    if "v" in phases:
        with contextlib.ExitStack() as es:
            sb = lambda n, shape, dt: gsb(es, n, shape, dt)
            wv_sb = sb("wv_sb", [128, NDMC * HID], BF16)
            for dmc in range(NDMC):
                def ldwv(e, dmc=dmc):
                    return e.dma_start(out=wv_sb[:, dmc * HID:(dmc + 1) * HID],
                                       in_=wv[dmc * 128:(dmc + 1) * 128, :])
                S.dma("pool", ldwv, reads=[], writes=["wv_sb"])
            xt_sb = [sb("xt_sb%d" % i, [128, NDMC * 128], BF16) for i in range(2)]
            vstage = [sb("vstage%d" % i, [128, 512], BF16) for i in range(4)]
            for tc in range(NKC):
                i = tc % 2
                src = xt[:, tc * 128:(tc + 1) * 128].rearrange("(c p) f -> p c f", p=128)

                def ldxt(e, src=src, i=i):
                    return e.dma_start(out=xt_sb[i][:, :].rearrange("p (c f) -> p c f", f=128), in_=src)
                S.dma("pool", ldxt, reads=[], writes=["xts%d" % i])
                for hg in range(4):
                    pres = "pb%d" % hg
                    pb = PB[hg]

                    def mm(e, i=i, hg=hg, pb=pb):
                        last = None
                        for dmc in range(NDMC):
                            last = e.matmul(pb[:, :], xt_sb[i][:, dmc * 128:(dmc + 1) * 128],
                                            wv_sb[:, dmc * HID + hg * 512: dmc * HID + hg * 512 + 512],
                                            start=(dmc == 0), stop=(dmc == NDMC - 1))
                        return last
                    S.op("pe", mm, reads=["xts%d" % i, "wv_sb"], writes=[pres])
                    vst = (tc % 2) * 2 + hg % 2

                    def ev(e, pb=pb, vst=vst):
                        return e.activation(vstage[vst][:, :], pb[:, :], ACT.Copy)
                    S.op("act", ev, reads=[pres], writes=["vstage%d" % vst])
                    dst = vsp[hg * 4: hg * 4 + 4, tc, :, :]

                    def spv(e, dst=dst, vst=vst):
                        return e.dma_start(out=dst.rearrange("h p f -> p h f"),
                                           in_=vstage[vst][:, :].rearrange("p (h f) -> p h f", f=128))
                    S.dma("pool", spv, reads=["vstage%d" % vst], writes=["vsp_%d_%d" % (tc, hg)],
                          sem="vsps%d" % vst)
                dq_step(1)
        S.fence()

    RH = 2
    if "attn" in phases:
        with contextlib.ExitStack() as es:
            sb = lambda n, shape, dt: gsb(es, n, shape, dt)
            gb_sb = sb("gb_sb", [128, 512], F32)
            bb_sb = sb("bb_sb", [128, 512], F32)
            mA_sb = sb("mA_sb", [128, 128], F32)
            mB_sb = sb("mB_sb", [128, 128], F32)
            S.dma("sp", lambda e: e.dma_start(out=gb_sb[:, :], in_=gb2[:, :]), reads=[], writes=["gb_sb"])
            S.dma("sp", lambda e: e.dma_start(out=bb_sb[:, :], in_=bb2[:, :]), reads=[], writes=["bb_sb"])
            S.dma("sp", lambda e: e.dma_start(out=mA_sb[:, :], in_=mAd[:, :]), reads=[], writes=["mA_sb"])
            S.dma("sp", lambda e: e.dma_start(out=mB_sb[:, :], in_=mBd[:, :]), reads=[], writes=["mB_sb"])

            kfm = [sb("kfm%d" % j, [128, TKV], BF16) for j in range(RH)]
            qfm = [sb("qfm%d" % j, [128, 2048], BF16) for j in range(RH)]
            vaug = [sb("vaug%d" % j, [128, NKC * 129], BF16) for j in range(RH)]
            for j in range(RH):
                def ms(e, j=j):
                    return e.memset(vaug[j][:, :], 1.0)
                S.op("dve", ms, reads=[], writes=["vaug%d" % j])
            wfsb = {"k": sb("wfsb_k", [128, NDMC * 128 * RH], BF16),
                    "q": sb("wfsb_q", [128, NDMC * 128 * RH], BF16)}
            wslc = [sb("wslc%d" % i, [128, 128 * RH], BF16) for i in range(2)]
            xts2 = [sb("xts2_%d" % i, [128, NDMC * 128], BF16) for i in range(2)]
            ygrp = sb("ygrp", [128, 8 * 128 * RH], F32)
            y2scr = sb("y2scr", [128, 128 * RH], F32)
            ktmp = sb("ktmp", [128, 128 * RH], BF16)
            ssum = sb("ssum", [128, 8 * RH], F32)
            ssq = sb("ssq", [128, 8 * RH], F32)
            smu = sb("smu", [128, 8 * RH], F32)
            srs = sb("srs", [128, 8 * RH], F32)
            sscr = sb("sscr", [128, 8 * RH], F32)
            kmu = sb("kmu", [128, NKC * RH], F32)
            krs = sb("krs", [128, NKC * RH], F32)
            qmu = sb("qmu", [128, NQC * RH], F32)
            qrs = sb("qrs", [128, NQC * RH], F32)
            asb = [sb("asb%d" % i, [128, 512], BF16) for i in range(2)]
            onorm = sb("onorm", [128, 128], BF16)
            ofm = [sb("ofm%d" % j, [128, 2048], BF16) for j in range(RH)]
            zeps = sb("zeps", [128, 1], F32)
            zinv = sb("zinv", [128, 1], F32)
            W = 128 * RH

            def proj_fm(r, which, wdram, src_dram, nchunks, dst_list, mubuf, rsbuf):
                for dmc in range(NDMC):
                    i = dmc % 2

                    def ldw(e, dmc=dmc, i=i, wdram=wdram, r=r):
                        return e.dma_start(out=wslc[i][:, :],
                                           in_=wdram[dmc * 128:(dmc + 1) * 128, r * W:(r + 1) * W])
                    S.dma("pool", ldw, reads=["wslc_free%d" % i], writes=["wslc%d" % i])

                    def fold(v, dmc=dmc, i=i, which=which):
                        return v.tensor_tensor(wfsb[which][:, dmc * W:(dmc + 1) * W],
                                               wslc[i][:, :], gb_sb[:, :W], ALU.mult)
                    S.op("dve", fold, reads=["wslc%d" % i, "gb_sb"],
                         writes=["wfsb_" + which, "wslc_free%d" % i])
                ngrp = nchunks // 8
                for grp in range(ngrp):
                    for tcg in range(8):
                        tc = grp * 8 + tcg
                        i = tc % 2
                        ssrc = src_dram[:, tc * 128:(tc + 1) * 128].rearrange("(c p) f -> p c f", p=128)

                        def ldx(e, ssrc=ssrc, i=i):
                            return e.dma_start(out=xts2[i][:, :].rearrange("p (c f) -> p c f", f=128),
                                               in_=ssrc)
                        S.dma("pool", ldx, reads=["xts2free%d" % i], writes=["xts2_%d" % i])
                        pres = ["pb4", "pb5", "pb3"][tc % 3]
                        pb = PB[[4, 5, 3][tc % 3]]

                        def mm(e, i=i, pb=pb, which=which):
                            last = None
                            for dmc in range(NDMC):
                                last = e.matmul(pb[:, 0:W], xts2[i][:, dmc * 128:(dmc + 1) * 128],
                                                wfsb[which][:, dmc * W:(dmc + 1) * W],
                                                start=(dmc == 0), stop=(dmc == NDMC - 1))
                            return last
                        S.op("pe", mm, reads=["xts2_%d" % i, "wfsb_" + which],
                             writes=[pres, "xts2free%d" % i])
                        for j in range(RH):
                            col = tcg * RH + j

                            def badd(e, pb=pb, j=j, tcg=tcg, col=col):
                                return e.scalar_tensor_tensor(
                                    ygrp[:, tcg * W + j * 128: tcg * W + j * 128 + 128],
                                    pb[:, j * 128:(j + 1) * 128], 1.0,
                                    bb_sb[:, j * 128:(j + 1) * 128], ALU.mult, ALU.add,
                                    accum_out=ssum[:, col:col + 1])
                            S.op("dve", badd, reads=[pres, "bb_sb"], writes=["yg_%d_%d" % (tcg, j)])

                            def sqa(e, tcg=tcg, j=j, col=col):
                                return e.activation(y2scr[:, j * 128:(j + 1) * 128],
                                                    ygrp[:, tcg * W + j * 128: tcg * W + j * 128 + 128],
                                                    ACT.Square, accum_out=ssq[:, col:col + 1])
                            S.op("act", sqa, reads=["yg_%d_%d" % (tcg, j)], writes=["sq_%d_%d" % (tcg, j)])
                    ygres = ["yg_%d_%d" % (t_, j_) for t_ in range(8) for j_ in range(RH)]
                    sqres = ["sq_%d_%d" % (t_, j_) for t_ in range(8) for j_ in range(RH)]
                    NC8 = 8 * RH

                    def st1(v):
                        return v.scalar_tensor_tensor(sscr[:, :NC8], ssum[:, :NC8], 1.0 / 128.0,
                                                      ssum[:, :NC8], ALU.mult, ALU.mult)
                    S.op("dve", st1, reads=ygres, writes=["sscr"])

                    def st2(v):
                        return v.tensor_tensor(sscr[:, :NC8], ssq[:, :NC8], sscr[:, :NC8],
                                               ALU.subtract)
                    S.op("dve", st2, reads=sqres + ["sscr"], writes=["sscr"])

                    def st3(v):
                        return v.tensor_scalar(sscr[:, :NC8], sscr[:, :NC8], 1.0 / 128.0, EPS,
                                               ALU.mult, ALU.add)
                    S.op("dve", st3, reads=["sscr"], writes=["sscr"])

                    def st4(v):
                        return v.reciprocal(srs[:, :NC8], sscr[:, :NC8])
                    S.op("dve", st4, reads=["sscr"], writes=["srs_p"])

                    def st5(v):
                        return v.tensor_scalar(smu[:, :NC8], ssum[:, :NC8], 1.0 / 128.0, None,
                                               ALU.mult)
                    S.op("dve", st5, reads=ygres, writes=["smu"])

                    def rsq(sc):
                        return sc.activation(srs[:, :NC8], srs[:, :NC8], ACT.Sqrt)
                    S.op("act", rsq, reads=["srs_p"], writes=["srs"])
                    if which == "q":
                        def dsc_(v):
                            return v.tensor_scalar(srs[:, :NC8], srs[:, :NC8], DSC, None, ALU.mult)
                        S.op("dve", dsc_, reads=["srs"], writes=["srs"])

                    def stcopy(v, grp=grp, mubuf=mubuf, rsbuf=rsbuf):
                        v.tensor_copy(rsbuf[:, grp * NC8:(grp + 1) * NC8], srs[:, :NC8])
                        return v.scalar_tensor_tensor(mubuf[:, grp * NC8:(grp + 1) * NC8],
                                                      smu[:, :NC8], -1.0, srs[:, :NC8],
                                                      ALU.mult, ALU.mult)
                    S.op("dve", stcopy, reads=["smu", "srs"], writes=[which + "must", "ssum_free"])
                    for tcg in range(8):
                        tc = grp * 8 + tcg
                        for j in range(RH):
                            col = grp * NC8 + tcg * RH + j

                            def appl(sc, tcg=tcg, j=j, col=col, mubuf=mubuf, rsbuf=rsbuf):
                                return sc.activation(ktmp[:, j * 128:(j + 1) * 128],
                                                     ygrp[:, tcg * W + j * 128: tcg * W + j * 128 + 128],
                                                     ACT.Identity, bias=mubuf[:, col:col + 1],
                                                     scale=rsbuf[:, col:col + 1])
                            S.op("act", appl, reads=["yg_%d_%d" % (tcg, j), which + "must"],
                                 writes=["ktmp%d" % j])
                            bank = 6 + (tcnt[0] % 2)
                            tcnt[0] += 1
                            pres = "pb%d" % bank

                            def tr(e, j=j, bank=bank):
                                return e.transpose(PBv[bank][:, 0:128], ktmp[:, j * 128:(j + 1) * 128],
                                                   ident16[:, :])
                            S.op("pe", tr, reads=["ktmp%d" % j, "ident16"], writes=[pres])

                            def cp(e, bank=bank, j=j, tc=tc, dst_list=dst_list):
                                return e.activation(dst_list[j][:, tc * 128:(tc + 1) * 128],
                                                    PBv[bank][:, 0:128], ACT.Copy)
                            S.op("act", cp, reads=[pres], writes=["dfm_%s%d" % ("kq"[dst_list is qfm], j)])

            for r in range(NH // RH):
                proj_fm(r, "k", wk, xt, NKC, kfm, kmu, krs)
                proj_fm(r, "q", wq, xqf, NQC, qfm, qmu, qrs)
                for j in range(RH):
                    hh = r * RH + j
                    src = vsp[hh, :, :, :]
                    dst = vaug[j][:, :].rearrange("p (c f) -> p c f", f=129)[:, :, 0:128]

                    def ldv(e, src=src, dst=dst):
                        return e.dma_start(out=dst, in_=src.rearrange("c p f -> p c f"))
                    S.dma("pool", ldv, reads=["vaug_free%d" % j], writes=["vaug%d" % j])
                for j in range(RH):
                    hh = r * RH + j
                    for ci in range(NQC):
                        nkci = 2 * ci + 2
                        ngrp = (nkci + 3) // 4

                        def emit_sgrp(g):
                            k0, k1 = g * 4, min(g * 4 + 4, nkci)
                            spres = "pb%d" % (g % 2)
                            spb = PB[g % 2]

                            def smm(e, j=j, k0=k0, k1=k1, ci=ci, spb=spb):
                                last = None
                                for kc in range(k0, k1):
                                    last = e.matmul(spb[:, (kc % 4) * 128:(kc % 4) * 128 + 128],
                                                    kfm[j][:, kc * 128:(kc + 1) * 128],
                                                    qfm[j][:, ci * 128:(ci + 1) * 128],
                                                    start=True, stop=True)
                                return last
                            S.op("pe", smm, reads=["dfm_k%d" % j, "dfm_q%d" % j], writes=[spres])
                        emit_sgrp(0)
                        for g in range(ngrp):
                            if g + 1 < ngrp:
                                emit_sgrp(g + 1)
                            k0 = g * 4
                            k1 = min(k0 + 4, nkci)
                            w = (k1 - k0) * 128
                            spres = "pb%d" % (g % 2)
                            spb = PB[g % 2]

                            def sqev(e, spb=spb, w=w, g=g):
                                return e.activation(asb[g % 2][:, :w], spb[:, :w], ACT.Square)
                            S.op("act", sqev, reads=[spres], writes=["asb%d" % (g % 2), spres + "f"])
                            for kc in range(k0, k1):
                                if kc == nkci - 2:
                                    def mka(v, kc=kc, g=g):
                                        return v.tensor_tensor(
                                            asb[g % 2][:, (kc % 4) * 128:(kc % 4) * 128 + 128],
                                            asb[g % 2][:, (kc % 4) * 128:(kc % 4) * 128 + 128],
                                            mA_sb[:, :], ALU.mult)
                                    S.op("dve", mka, reads=["asb%d" % (g % 2), "mA_sb"],
                                         writes=["asb%d" % (g % 2)])
                                elif kc == nkci - 1:
                                    def mkb(v, kc=kc, g=g):
                                        return v.tensor_tensor(
                                            asb[g % 2][:, (kc % 4) * 128:(kc % 4) * 128 + 128],
                                            asb[g % 2][:, (kc % 4) * 128:(kc % 4) * 128 + 128],
                                            mB_sb[:, :], ALU.mult)
                                    S.op("dve", mkb, reads=["asb%d" % (g % 2), "mB_sb"],
                                         writes=["asb%d" % (g % 2)])
                            def avmm(e, j=j, k0=k0, k1=k1, g=g, nkci=nkci):
                                last = None
                                for kc in range(k0, k1):
                                    last = e.matmul(PB[2][:, 0:129],
                                                    asb[g % 2][:, (kc % 4) * 128:(kc % 4) * 128 + 128],
                                                    vaug[j][:, kc * 129: kc * 129 + 129],
                                                    start=(kc == 0), stop=(kc == nkci - 1))
                                return last
                            S.op("pe", avmm, reads=["asb%d" % (g % 2), "vaug%d" % j], writes=["pb2"])

                        def zep(sc):
                            return sc.activation(zeps[:, :], PB[2][:, 128:129], ACT.Copy, bias=EPS)
                        S.op("act", zep, reads=["pb2"], writes=["zeps"])

                        def zev(v):
                            return v.reciprocal(zinv[:, :], zeps[:, :])
                        S.op("dve", zev, reads=["zeps"], writes=["zinv"])

                        def onor(sc):
                            return sc.activation(onorm[:, :], PB[2][:, 0:128], ACT.Copy,
                                                 scale=zinv[:, :])
                        S.op("act", onor, reads=["pb2", "zinv"], writes=["onorm", "pb2f"])
                        bank = 6 + (tcnt[0] % 2)
                        tcnt[0] += 1
                        pres = "pb%d" % bank

                        def otr(e, bank=bank):
                            return e.transpose(PBv[bank][:, 0:128], onorm[:, :], ident16[:, :])
                        S.op("pe", otr, reads=["onorm", "ident16"], writes=[pres])

                        def ocp(e, bank=bank, j=j, ci=ci):
                            return e.activation(ofm[j][:, ci * 128:(ci + 1) * 128],
                                                PBv[bank][:, 0:128], ACT.Copy)
                        S.op("act", ocp, reads=[pres], writes=["ofm%d" % j])
                        dq_step(1)

                    def spo(e, j=j, hh=hh):
                        return e.dma_start(out=osp[hh, :, :], in_=ofm[j][:, :])
                    S.dma("pool", spo, reads=["ofm%d" % j], writes=["osp%d" % hh, "vaug_free%d" % j],
                          sem="osps%d" % j)
        S.fence()
    dq_step(400)
    S.fence(include_nofence=True)
    dq_es.close()

    if "wo" in phases:
        with contextlib.ExitStack() as es:
            sb = lambda n, shape, dt: gsb(es, n, shape, dt)
            wo_sb = sb("wo_sb", [128, NDMC * HID], BF16)
            for oc in range(NDMC):
                def ldwo(e, oc=oc):
                    return e.dma_start(out=wo_sb[:, oc * HID:(oc + 1) * HID],
                                       in_=wo[oc * 128:(oc + 1) * 128, :])
                S.dma("pool", ldwo, reads=[], writes=["wo_sb"])
            oq = [sb("oq%d" % i, [128, 2048], BF16) for i in range(2)]
            xts3 = [sb("xts3_%d" % i, [128, 2048], F32) for i in range(2)]
            rstage = sb("rstage", [128, 2048], F32)
            r2scr = sb("r2scr", [128, 512], F32)
            lsum = sb("lsum", [128, 4], F32)
            lsq = sb("lsq", [128, 4], F32)
            lmu = sb("lmu", [128, 1], F32)
            lrs = sb("lrs", [128, 1], F32)
            lscr = sb("lscr", [128, 1], F32)
            x1t = sb("x1t", [128, 2048], F32)
            for qc in range(NQC):
                i = qc % 2
                bank0 = (qc % 2) * 4
                src = osp[:, :, qc * 128:(qc + 1) * 128]

                def ldo(e, src=src, i=i):
                    return e.dma_start(out=oq[i][:, :].rearrange("p (h f) -> p h f", f=128),
                                       in_=src.rearrange("h p f -> p h f"))
                S.dma("sp", ldo, reads=["oqfree%d" % i], writes=["oq%d" % i])

                def ldxq(e, qc=qc, i=i):
                    return e.dma_start(out=xts3[i][:, :], in_=xqt[qc * 128:(qc + 1) * 128, :])
                S.dma("sp", ldxq, reads=["xts3free%d" % i], writes=["xts3_%d" % i])
                for dmt in range(4):
                    pres = "pb%d" % (bank0 + dmt)
                    pb = PB[bank0 + dmt]

                    def mm(e, i=i, pb=pb, dmt=dmt):
                        last = None
                        for hh in range(NH):
                            last = e.matmul(pb[:, :], oq[i][:, hh * 128:(hh + 1) * 128],
                                            wo_sb[:, hh * HID + dmt * 512: hh * HID + dmt * 512 + 512],
                                            start=(hh == 0), stop=(hh == NH - 1))
                        return last
                    S.op("pe", mm, reads=["oq%d" % i, "wo_sb"],
                         writes=[pres] + (["oqfree%d" % i] if dmt == 3 else []))

                    def radd(e, pb=pb, dmt=dmt, i=i):
                        return e.scalar_tensor_tensor(rstage[:, dmt * 512:(dmt + 1) * 512], pb[:, :], 1.0,
                                                      xts3[i][:, dmt * 512:(dmt + 1) * 512], ALU.mult, ALU.add,
                                                      accum_out=lsum[:, dmt:dmt + 1])
                    S.op("dve", radd, reads=[pres, "xts3_%d" % i],
                         writes=["rstage%d" % dmt] + (["xts3free%d" % i] if dmt == 3 else []))

                    def rsq(e, dmt=dmt):
                        return e.activation(r2scr[:, :], rstage[:, dmt * 512:(dmt + 1) * 512], ACT.Square,
                                            accum_out=lsq[:, dmt:dmt + 1])
                    S.op("act", rsq, reads=["rstage%d" % dmt], writes=["lsq%d" % dmt])

                def ln1(v):
                    return v.tensor_reduce(lmu[:, :], lsum[:, :], AX.X, ALU.add)
                S.op("dve", ln1, reads=["rstage%d" % d_ for d_ in range(4)], writes=["lmu"])

                def ln2(v):
                    return v.tensor_reduce(lscr[:, :], lsq[:, :], AX.X, ALU.add)
                S.op("dve", ln2, reads=["lsq%d" % d_ for d_ in range(4)], writes=["lscr"])

                def ln3(v):
                    return v.scalar_tensor_tensor(lrs[:, :], lmu[:, :], 1.0 / 2048.0,
                                                  lmu[:, :], ALU.mult, ALU.mult)
                S.op("dve", ln3, reads=["lmu"], writes=["lrs_t"])

                def ln4(v):
                    return v.tensor_tensor(lscr[:, :], lscr[:, :], lrs[:, :], ALU.subtract)
                S.op("dve", ln4, reads=["lscr", "lrs_t"], writes=["lscr"])

                def ln5(v):
                    return v.tensor_scalar(lscr[:, :], lscr[:, :], 1.0 / 2048.0, EPS,
                                           ALU.mult, ALU.add)
                S.op("dve", ln5, reads=["lscr"], writes=["lscr"])

                def ln6(v):
                    return v.reciprocal(lrs[:, :], lscr[:, :])
                S.op("dve", ln6, reads=["lscr"], writes=["lrs_p"])

                def ln7(v):
                    return v.tensor_scalar(lmu[:, :], lmu[:, :], 1.0 / 2048.0, None, ALU.mult)
                S.op("dve", ln7, reads=["lmu"], writes=["lmu"])

                def lnsq(sc):
                    return sc.activation(lrs[:, :], lrs[:, :], ACT.Sqrt)
                S.op("act", lnsq, reads=["lrs_p"], writes=["lrs"])

                def applb(v):
                    return v.scalar_tensor_tensor(lscr[:, :], lmu[:, :], -1.0, lrs[:, :],
                                                  ALU.mult, ALU.mult)
                S.op("dve", applb, reads=["lmu", "lrs"], writes=["lnb"])

                def appl(sc):
                    return sc.activation(x1t[:, :], rstage[:, :], ACT.Identity,
                                         bias=lscr[:, :], scale=lrs[:, :])
                S.op("act", appl, reads=["rstage%d" % d_ for d_ in range(4)] + ["lnb", "lrs"],
                     writes=["x1t"])

                def spx1(e, qc=qc):
                    return e.dma_start(out=x1s[qc, :, :], in_=x1t[:, :])
                S.dma("pool", spx1, reads=["x1t"], writes=["x1sp%d" % qc], sem="x1sp")
        S.fence()

    if "ffn" in phases:
        with contextlib.ExitStack() as es:
            sb = lambda n, shape, dt: gsb(es, n, shape, dt)
            qfs = [alloc_quant_bufs(es, nc, 512, "qf%d" % i) for i in range(2)]
            x1r = sb("x1r", [128, 2048], F32)
            x1res = x1r
            xdq_t = sb("xdq_t", [128, 2048], BF16)
            xdq = sb("xdq", [128, NDMC * 512], BF16)
            gslab = [sb("gslab%d" % i, [128, NDMC * 512], BF16) for i in range(2)]
            uslab = [sb("uslab%d" % i, [128, NDMC * 512], BF16) for i in range(2)]
            dslab = [sb("dslab%d" % i, [128, HID], BF16) for i in range(2)]
            hbuf = [sb("hbuf%d" % i, [128, 512], F32) for i in range(2)]
            hdq_t = [sb("hdq_t%d" % i, [128, 512], BF16) for i in range(2)]
            hdq = sb("hdq", [128, NIC * 512], BF16)
            f2scr = hbuf[0]
            fsum = sb("fsum", [128, 4], F32)
            fsq = sb("fsq", [128, 4], F32)
            fmu = sb("fmu", [128, 1], F32)
            frs = sb("frs", [128, 1], F32)
            fscr = sb("fscr", [128, 1], F32)
            outt = sb("outt", [128, 2048], F32)

            for tile in range(4):
                for qc4 in range(4):
                    qc = tile * 4 + qc4

                    def ldx1(e, qc=qc):
                        return e.dma_start(out=x1r[:, :], in_=x1s[qc, :, :])
                    S.dma("sp", ldx1, reads=["x1rfree"], writes=["x1r"])
                    for h in range(4):
                        emit_quant(S, qfs[h % 2], "qf%d" % (h % 2), x1r[:, h * 512:(h + 1) * 512], "x1r",
                                   xdq_t[:, h * 512:(h + 1) * 512], "xdq_t", 512)
                    for dmc in range(NDMC):
                        bank = 6 + (tcnt[0] % 2)
                        tcnt[0] += 1
                        pres = "pb%d" % bank

                        def tr(e, dmc=dmc, bank=bank):
                            return e.transpose(PBv[bank][:, 0:128], xdq_t[:, dmc * 128:(dmc + 1) * 128],
                                               ident16[:, :])
                        S.op("pe", tr, reads=["xdq_t", "ident16"],
                             writes=[pres] + (["x1rfree"] if dmc == NDMC - 1 else []))

                        def cp(e, bank=bank, dmc=dmc, qc4=qc4):
                            return e.activation(xdq[:, dmc * 512 + qc4 * 128: dmc * 512 + qc4 * 128 + 128],
                                                PBv[bank][:, 0:128], ACT.Copy)
                        S.op("act", cp, reads=[pres], writes=["xdq"])
                for ig in range(16):
                    sl = ig % 2
                    for nm, slab, wspl in (("g", gslab, gws), ("u", uslab, uws)):
                        src = wspl[:, :, ig * 512:(ig + 1) * 512]
                        for piece in range(4):
                            def ldsl(e, src=src, slab=slab, piece=piece, sl=sl):
                                return e.dma_start(
                                    out=slab[sl][:, piece * 4 * 512:(piece + 1) * 4 * 512].rearrange(
                                        "p (c f) -> p c f", f=512),
                                    in_=src[piece * 4:(piece + 1) * 4].rearrange("c p f -> p c f"))
                            S.dma("sp", ldsl, reads=[nm + "slabfree%d" % sl], writes=[nm + "slab%d" % sl])
                    for ph in range(2):
                        for q2 in range(2):
                            qc4 = 2 * ph + q2
                            presg = "pb%d" % (q2 * 2)
                            presu = "pb%d" % (q2 * 2 + 1)
                            pbg = PB[q2 * 2]
                            pbu = PB[q2 * 2 + 1]
                            for dmc4 in range(4):
                                def mmg(e, dmc4=dmc4, qc4=qc4, pbg=pbg, pbu=pbu, sl=sl):
                                    last = None
                                    for dmc in range(dmc4 * 4, dmc4 * 4 + 4):
                                        xsl = xdq[:, dmc * 512 + qc4 * 128: dmc * 512 + qc4 * 128 + 128]
                                        e.matmul(pbg[:, :], xsl,
                                                 gslab[sl][:, dmc * 512:(dmc + 1) * 512],
                                                 start=(dmc == 0), stop=(dmc == NDMC - 1))
                                        last = e.matmul(pbu[:, :], xsl,
                                                        uslab[sl][:, dmc * 512:(dmc + 1) * 512],
                                                        start=(dmc == 0), stop=(dmc == NDMC - 1))
                                    return last
                                S.op("pe", mmg,
                                     reads=["xdq", "gslab%d" % sl, "uslab%d" % sl],
                                     writes=[presg, presu] + (["gslabfree%d" % sl, "uslabfree%d" % sl]
                                                              if (dmc4 == 3 and ph == 1 and q2 == 1)
                                                              else []))
                        for q2 in range(2):
                            qc4 = 2 * ph + q2
                            presg = "pb%d" % (q2 * 2)
                            presu = "pb%d" % (q2 * 2 + 1)
                            pbg = PB[q2 * 2]
                            pbu = PB[q2 * 2 + 1]

                            def silu(e, pbg=pbg, q2=q2):
                                return e.activation(hbuf[q2][:, :], pbg[:, :], ACT.Silu)
                            S.op("act", silu, reads=[presg], writes=["hbuf%d" % q2, presg + "f"])

                            def hmul(v, pbu=pbu, q2=q2):
                                return v.tensor_tensor(hbuf[q2][:, :], hbuf[q2][:, :],
                                                       pbu[:, :], ALU.mult)
                            S.op("dve", hmul, reads=["hbuf%d" % q2, presu],
                                 writes=["hbuf%d" % q2, presu + "f"])
                        emit_quant_multi(S, [
                            (qfs[q2], "qf%d" % q2, hbuf[q2][:, :512], "hbuf%d" % q2,
                             hdq_t[q2][:, :512], "hdq_t%d" % q2, 512)
                            for q2 in range(2)])
                        for q2 in range(2):
                            qc4 = 2 * ph + q2
                            for t in range(4):
                                ic = ig * 4 + t
                                bank = 6 + (tcnt[0] % 2)
                                tcnt[0] += 1
                                pres = "pb%d" % bank

                                def tr(e, t=t, bank=bank, q2=q2):
                                    return e.transpose(PBv[bank][:, 0:128],
                                                       hdq_t[q2][:, t * 128:(t + 1) * 128],
                                                       ident16[:, :])
                                S.op("pe", tr, reads=["hdq_t%d" % q2, "ident16"], writes=[pres])

                                def cp(e, bank=bank, ic=ic, qc4=qc4):
                                    return e.activation(
                                        hdq[:, ic * 512 + qc4 * 128: ic * 512 + qc4 * 128 + 128],
                                        PBv[bank][:, 0:128], ACT.Copy)
                                S.op("act", cp, reads=[pres], writes=["hdq"])
                for half in range(2):
                    for ic in range(NIC):
                        i = ic % 2

                        def lddw(e, ic=ic, i=i):
                            return e.dma_start(out=dslab[i][:, :], in_=dws[ic, :, :])
                        S.dma("sp", lddw, reads=["dslabfree%d" % i], writes=["dslab%d" % i])
                        for q2 in range(2):
                            qc4 = half * 2 + q2

                            def mmd(e, ic=ic, i=i, qc4=qc4, q2=q2):
                                last = None
                                for dmt in range(4):
                                    last = e.matmul(
                                        PB[q2 * 4 + dmt][:, :],
                                        hdq[:, ic * 512 + qc4 * 128: ic * 512 + qc4 * 128 + 128],
                                        dslab[i][:, dmt * 512:(dmt + 1) * 512],
                                        start=(ic == 0), stop=(ic == NIC - 1))
                                return last
                            S.op("pe", mmd, reads=["hdq", "dslab%d" % i],
                                 writes=["pbd%d" % q2] + (["dslabfree%d" % i] if q2 == 1 else []))
                    for q2 in range(2):
                        qc4 = half * 2 + q2
                        qc = tile * 4 + qc4

                        def ldx1b(e, qc=qc):
                            return e.dma_start(out=x1res[:, :], in_=x1s[qc, :, :])
                        S.dma("sp", ldx1b, reads=["x1rfree"], writes=["x1r"])
                        for dmt in range(4):
                            pb = PB[q2 * 4 + dmt]

                            def radd(e, pb=pb, dmt=dmt):
                                return e.scalar_tensor_tensor(
                                    outt[:, dmt * 512:(dmt + 1) * 512], pb[:, :], 1.0,
                                    x1res[:, dmt * 512:(dmt + 1) * 512], ALU.mult, ALU.add,
                                    accum_out=fsum[:, dmt:dmt + 1])
                            S.op("dve", radd, reads=["pbd%d" % q2, "x1r"],
                                 writes=["outt%d" % dmt] + (["x1rfree"] if dmt == 3 else []))

                            def rsq(e, dmt=dmt):
                                return e.activation(f2scr[:, :], outt[:, dmt * 512:(dmt + 1) * 512],
                                                    ACT.Square, accum_out=fsq[:, dmt:dmt + 1])
                            S.op("act", rsq, reads=["outt%d" % dmt], writes=["fsq%d" % dmt])

                        def fn1(v):
                            return v.tensor_reduce(fmu[:, :], fsum[:, :], AX.X, ALU.add)
                        S.op("dve", fn1, reads=["outt%d" % d_ for d_ in range(4)], writes=["fmu"])

                        def fn2(v):
                            return v.tensor_reduce(fscr[:, :], fsq[:, :], AX.X, ALU.add)
                        S.op("dve", fn2, reads=["fsq%d" % d_ for d_ in range(4)], writes=["fscr"])

                        def fn3(v):
                            return v.scalar_tensor_tensor(frs[:, :], fmu[:, :], 1.0 / 2048.0,
                                                          fmu[:, :], ALU.mult, ALU.mult)
                        S.op("dve", fn3, reads=["fmu"], writes=["frs_t"])

                        def fn4(v):
                            return v.tensor_tensor(fscr[:, :], fscr[:, :], frs[:, :], ALU.subtract)
                        S.op("dve", fn4, reads=["fscr", "frs_t"], writes=["fscr"])

                        def fn5(v):
                            return v.tensor_scalar(fscr[:, :], fscr[:, :], 1.0 / 2048.0, EPS,
                                                   ALU.mult, ALU.add)
                        S.op("dve", fn5, reads=["fscr"], writes=["fscr"])

                        def fn6(v):
                            return v.reciprocal(frs[:, :], fscr[:, :])
                        S.op("dve", fn6, reads=["fscr"], writes=["frs_p"])

                        def fn7(v):
                            return v.tensor_scalar(fmu[:, :], fmu[:, :], 1.0 / 2048.0, None,
                                                   ALU.mult)
                        S.op("dve", fn7, reads=["fmu"], writes=["fmu"])

                        def lnsq(sc):
                            return sc.activation(frs[:, :], frs[:, :], ACT.Sqrt)
                        S.op("act", lnsq, reads=["frs_p"], writes=["frs"])

                        def applb(v):
                            return v.scalar_tensor_tensor(fscr[:, :], fmu[:, :], -1.0, frs[:, :],
                                                          ALU.mult, ALU.mult)
                        S.op("dve", applb, reads=["fmu", "frs"], writes=["fnb"])

                        def appl(sc):
                            return sc.activation(outt[:, :], outt[:, :], ACT.Identity,
                                                 bias=fscr[:, :], scale=frs[:, :])
                        S.op("act", appl, reads=["outt%d" % d_ for d_ in range(4)] + ["fnb", "frs"],
                             writes=["outt0", "outt1", "outt2", "outt3"])

                        def spout(e, qc=qc):
                            return e.dma_start(out=out[qc * 128:(qc + 1) * 128, :], in_=outt[:, :])
                        S.dma("pool", spout, reads=["outt%d" % d_ for d_ in range(4)],
                              writes=["outd%d" % qc], sem="outsp")
        S.fence()

    o = S.op("sp", None, reads=[], writes=[])
    o.waits = S.final_waits()
    o.deps = set()

    S.emit(nc, root)
    root.close()
    return nc


def core_inputs(core, x, Wq, Wk, Wv, Wo, gamma, beta, gate_w, up_w, down_w):
    b = core // 2
    half = core % 2
    xb = np.ascontiguousarray(x[b])
    xtb = np.ascontiguousarray(xb.T)
    qcols = np.concatenate([np.arange(c * 128, (c + 1) * 128)
                            for c in range(half, 32, 2)])
    tri = np.triu(np.ones((128, 128), dtype=np.float32))
    if half == 0:
        mA, mB = tri, np.zeros((128, 128), np.float32)
    else:
        mA, mB = np.ones((128, 128), np.float32), tri
    d = {
        "xt": xtb,
        "xqf": np.ascontiguousarray(xtb[:, qcols]),
        "xqt": np.ascontiguousarray(xb[qcols]),
        "wqT": np.ascontiguousarray(Wq.T),
        "wkT": np.ascontiguousarray(Wk.T),
        "wvT": np.ascontiguousarray(Wv.T),
        "woT": np.ascontiguousarray(Wo.T),
        "gw": np.ascontiguousarray(gate_w),
        "uw": np.ascontiguousarray(up_w),
        "dw": np.ascontiguousarray(down_w),
        "gb2": np.ascontiguousarray(np.broadcast_to(np.tile(gamma, 4)[None, :], (128, 512))),
        "bb2": np.ascontiguousarray(np.broadcast_to(np.tile(beta, 4)[None, :], (128, 512))),
        "mA": mA,
        "mB": mB,
        "idin": np.eye(128, dtype=np.float32),
    }
    return d, qcols


_NC_CACHE = {}


def _get_nc():
    if "nc" not in _NC_CACHE:
        _NC_CACHE["nc"] = build_program({})
    return _NC_CACHE["nc"]


def kernel(L0_attn_hidden_states, L0_attn_Wq, L0_attn_Wk, L0_attn_Wv, L0_attn_Wo,
           L0_attn_gamma, L0_attn_beta, L0_ffn_gate_w, L0_ffn_up_w, L0_ffn_down_w):
    from concourse.bass_utils import run_bass_kernel_spmd
    x = np.asarray(L0_attn_hidden_states, dtype=np.float32)
    args = (x, np.asarray(L0_attn_Wq, np.float32), np.asarray(L0_attn_Wk, np.float32),
            np.asarray(L0_attn_Wv, np.float32), np.asarray(L0_attn_Wo, np.float32),
            np.asarray(L0_attn_gamma, np.float32), np.asarray(L0_attn_beta, np.float32),
            np.asarray(L0_ffn_gate_w, np.float32), np.asarray(L0_ffn_up_w, np.float32),
            np.asarray(L0_ffn_down_w, np.float32))
    nc = _get_nc()
    ims = []
    qcls = []
    for c in range(8):
        d, qc = core_inputs(c, *args)
        ims.append(d)
        qcls.append(qc)
    res = run_bass_kernel_spmd(nc, ims, core_ids=list(range(8)))
    out = np.empty((4, 4096, 2048), dtype=np.float32)
    for c in range(8):
        out[c // 2][qcls[c]] = np.asarray(res.results[c]["outp"], dtype=np.float32)
    return out
